# version 11
# speedup vs baseline: 1.5717x; 1.0210x over previous
HARD_SYNC = False
"""Megakernel builder for mk_seq_postln_d2 (v1: correctness-first, strong waits).

Layouts:
  TM (token-major): [128 tokens on partitions, features on free]
  FM (feature-major): [128 features on partitions, tokens on free]
Per core c: batch b=c//2, half=c%2, own tokens [half*4096,(half+1)*4096),
extended range adds 128 tokens each side (zero-padded at batch edges,
invalid rows killed via mask columns in stages B/D).

Weights host-pre-transposed. Matmuls bf16 x bf16 -> fp32 PSUM.
"""
import numpy as np
import ml_dtypes
import concourse.bass as bass
import concourse.mybir as mybir
from concourse.alu_op_type import AluOpType

F32 = mybir.dt.float32
BF16 = mybir.dt.bfloat16
I32 = mybir.dt.int32
ACT = mybir.ActivationFunctionType
AX = mybir.AxisListType

D = 1024
DC = 8
S = 8192
TOWN = 4096
TEXT = 4352
NT = TEXT // 128

C_NEGHALF, C_NEGONE, C_INV32, C_EPS, C_INV1024, C_HALF, C_QCLIP, C_127, \
    C_INV127, C_ONE, C_ZERO, C_INVWELEM, C_SP1, C_SP2, C_SP3, C_SP4 = range(16)

CONSTS_NP = np.zeros((128, 16), np.float32)
CONSTS_NP[:, C_NEGHALF] = -0.5
CONSTS_NP[:, C_NEGONE] = -1.0
CONSTS_NP[:, C_INV32] = 1.0 / 32.0
CONSTS_NP[:, C_EPS] = 1e-5
CONSTS_NP[:, C_INV1024] = 1.0 / 1024.0
CONSTS_NP[:, C_HALF] = 0.5
CONSTS_NP[:, C_QCLIP] = 1e-5
CONSTS_NP[:, C_127] = 127.0
CONSTS_NP[:, C_INV127] = 1.0 / 127.0
CONSTS_NP[:, C_ONE] = 1.0
CONSTS_NP[:, C_INVWELEM] = 1.0 / (4096.0 * 1024.0)


class Ctx:

    def __init__(self, nc):
        self.nc = nc
        self.sems = {}
        self.counts = {}
        self.eng = {}
        self._stack = None
        self.suffix = ""

    def set_stage(self, suffix):
        self.suffix = suffix

    def sem(self, name):
        name = name + self.suffix
        if name not in self.sems:
            h = self._stack.enter_context(self.nc.semaphore(name))
            self.sems[name] = h
            self.counts[name] = 0
        return self.sems[name]

    def inc(self, inst, name, n=1):
        inst.then_inc(self.sem(name), n)
        self.counts[name + self.suffix] += n
        return self.counts[name + self.suffix]

    def hard_sync(self):
        for e in self.eng.values():
            for s in list(self.counts):
                if self.counts[s] > 0:
                    e.wait_ge(self.sems[s], self.counts[s])

    def stage_barrier(self):
        self.hard_sync()

    def val(self, name):
        self.sem(name)
        return self.counts[name + self.suffix]

    def after(self, consumer, *sem_names):
        for s0 in sem_names:
            s = s0 + self.suffix
            if s in self.counts and self.counts[s] > 0:
                consumer.wait_ge(self.sems[s], self.counts[s])

ALL_SEMS = ("d_sy", "d_gp", "s_te", "s_ve", "s_sc", "s_gp")


def ln_epilogue(ctx, ve, sc, x_ap, lncol, sqdump, consts_sb, out_ap):
    c = ve.tensor_reduce(out=lncol[:, 0:1], in_=x_ap, axis=AX.X, op=AluOpType.add)
    ctx.inc(c, "s_ve")
    ve.wait_ge(ctx.sem("s_ve"), ctx.counts["s_ve" + ctx.suffix])
    c = ve.tensor_scalar(out=lncol[:, 1:2], in0=lncol[:, 0:1],
                         scalar1=1.0 / 1024.0, scalar2=None, op0=AluOpType.mult)
    ctx.inc(c, "s_ve")
    ve.wait_ge(ctx.sem("s_ve"), ctx.counts["s_ve" + ctx.suffix])
    c = ve.tensor_scalar(out=x_ap, in0=x_ap, scalar1=lncol[:, 1:2], scalar2=None,
                         op0=AluOpType.subtract)
    ctx.inc(c, "s_ve")
    ctx.after(sc, "s_ve")
    a = sc.activation(out=sqdump, in_=x_ap, func=ACT.Square, accum_out=lncol[:, 2:3])
    ctx.inc(a, "s_sc")
    sc.wait_ge(ctx.sem("s_sc"), ctx.counts["s_sc" + ctx.suffix])
    a = sc.activation(out=lncol[:, 3:4], in_=lncol[:, 2:3], func=ACT.Sqrt,
                      scale=consts_sb[:, C_INV1024:C_INV1024 + 1],
                      bias=consts_sb[:, C_EPS:C_EPS + 1])
    ctx.inc(a, "s_sc")
    ctx.after(ve, "s_sc")
    c = ve.reciprocal(lncol[:, 4:5], lncol[:, 3:4])
    ctx.inc(c, "s_ve")
    ve.wait_ge(ctx.sem("s_ve"), ctx.counts["s_ve" + ctx.suffix])
    c = ve.tensor_scalar(out=out_ap, in0=x_ap, scalar1=lncol[:, 4:5], scalar2=None,
                         op0=AluOpType.mult)
    ctx.inc(c, "s_ve")


def emit_stage_a(ctx, io):
    nc = ctx.nc
    sync, gp, ve, sc, te = (ctx.eng[k] for k in ("sync", "gpsimd", "vector", "scalar", "tensor"))
    consts_sb = io["consts_sb"]

    with (
        nc.sbuf_tensor("a_wq", [128, 8 * 3072], BF16) as wq,
        nc.sbuf_tensor("a_bqc", [128, 24], F32) as bqc,
        nc.sbuf_tensor("a_bvr", [1, 1024], BF16) as bvr,
        nc.sbuf_tensor("a_tt", [128, 4 * 1024], F32) as t_tm,
        nc.sbuf_tensor("a_tb", [128, 4 * 1024], BF16) as t_bf,
        nc.sbuf_tensor("a_tT", [128, 8 * 512], BF16) as t_fm,
        nc.sbuf_tensor("a_ev", [128, 2 * 512], BF16) as ev,
        nc.sbuf_tensor("a_vev", [128, 2 * 1024], BF16) as vev,
        nc.psum_tensor("a_ps_t", [128, 512], BF16) as ps_t,
        nc.psum_tensor("a_ps_p", [128, 2 * 512], F32) as ps_p,
    ):
        for dc in range(DC):
            ctx.inc(gp.dma_start(out=wq[:, dc * 3072:(dc + 1) * 3072],
                                 in_=io["WqkvT"][dc * 128:(dc + 1) * 128, :]), "d_gp", 16)
        ctx.inc(sync.dma_start(out=bqc[:], in_=io["bqkv_cols"]), "d_sy", 16)
        ctx.inc(gp.dma_start(out=bvr[:], in_=io["bv_row"]), "d_gp", 16)
        ctx.after(te, "d_gp", "d_sy")
        ctx.after(sc, "d_gp", "d_sy")

        def proj_block(src_ap, row0, n_tt, do_q, do_kv, col0):
            nt128 = n_tt * 128
            ctx.after(sync, "s_sc")
            for i in range(n_tt):
                ctx.inc(sync.dma_start(out=t_tm[:, i * 1024:(i + 1) * 1024],
                                       in_=src_ap[row0 + i * 128: row0 + (i + 1) * 128, :]),
                        "d_sy", 16)
            ctx.after(sc, "d_sy", "s_te")
            for i in range(n_tt):
                a = sc.activation(out=t_bf[:, i * 1024:(i + 1) * 1024],
                                  in_=t_tm[:, i * 1024:(i + 1) * 1024], func=ACT.Copy)
            ctx.inc(a, "s_sc")
            ctx.after(te, "s_sc")
            for dc in range(DC):
                ctx.after(te, "s_ve")
                for i in range(n_tt):
                    m = te.transpose(ps_t[:, i * 128:(i + 1) * 128],
                                     t_bf[:, i * 1024 + dc * 128: i * 1024 + (dc + 1) * 128],
                                     io["ident_sb"])
                ctx.inc(m, "s_te")
                ctx.after(ve, "s_te")
                c = ve.tensor_copy(t_fm[:, dc * 512: dc * 512 + nt128], ps_t[:, 0:nt128])
                ctx.inc(c, "s_ve")
            ctx.after(te, "s_ve")

            def proj_fm(ci_feat0, bias_col, scr, scr_col0):
                mm_done = [None] * DC
                ev_done = [None] * DC
                dma_done = [None] * DC

                def emit_mm(ci):
                    pb = ci % 2
                    for dc in range(DC):
                        m = te.matmul(ps_p[:, pb * 512: pb * 512 + nt128],
                                      wq[:, dc * 3072 + ci_feat0 + ci * 128: dc * 3072 + ci_feat0 + (ci + 1) * 128],
                                      t_fm[:, dc * 512: dc * 512 + nt128],
                                      start=(dc == 0), stop=(dc == DC - 1))
                    mm_done[ci] = ctx.inc(m, "s_te")

                ctx.after(te, "d_sy", "s_sc")
                emit_mm(0)
                for ci in range(DC):
                    pb = ci % 2
                    if ci + 1 < DC:
                        if ci >= 1:
                            te.wait_ge(ctx.sem("s_sc"), ev_done[ci - 1])
                        emit_mm(ci + 1)
                    sc.wait_ge(ctx.sem("s_te"), mm_done[ci])
                    if ci >= 2:
                        sc.wait_ge(ctx.sem("d_sy"), dma_done[ci - 2])
                    a = sc.activation(out=ev[:, pb * 512: pb * 512 + nt128],
                                      in_=ps_p[:, pb * 512: pb * 512 + nt128],
                                      func=ACT.Identity, bias=bqc[:, bias_col + ci: bias_col + ci + 1])
                    ev_done[ci] = ctx.inc(a, "s_sc")
                    sync.wait_ge(ctx.sem("s_sc"), ev_done[ci])
                    dma_done[ci] = ctx.inc(sync.dma_start(
                        out=scr[ci * 128:(ci + 1) * 128, scr_col0: scr_col0 + nt128],
                        in_=ev[:, pb * 512: pb * 512 + nt128]), "d_sy", 16)

            if do_kv:
                proj_fm(1024, 8, io["Kscr"], col0)
                for i in range(n_tt):
                    pb = i % 2
                    ctx.after(te, "d_sy")
                    for vh in range(2):
                        for dc in range(DC):
                            m = te.matmul(ps_p[:, vh * 512:(vh + 1) * 512],
                                          t_fm[:, dc * 512 + i * 128: dc * 512 + (i + 1) * 128],
                                          wq[:, dc * 3072 + 2048 + vh * 512: dc * 3072 + 2048 + (vh + 1) * 512],
                                          start=(dc == 0), stop=False)
                        m = te.matmul(ps_p[:, vh * 512:(vh + 1) * 512],
                                      io["ones_col_sb"], bvr[:, vh * 512:(vh + 1) * 512],
                                      start=False, stop=True)
                    ctx.inc(m, "s_te")
                    ctx.after(ve, "s_te", "d_sy")
                    c = ve.tensor_copy(vev[:, pb * 1024:(pb + 1) * 1024], ps_p[:, 0:1024])
                    ctx.inc(c, "s_ve")
                    ctx.after(sync, "s_ve")
                    ctx.inc(sync.dma_start(
                        out=io["Vscr"][col0 + i * 128: col0 + (i + 1) * 128, :],
                        in_=vev[:, pb * 1024:(pb + 1) * 1024]), "d_sy", 16)
                    ctx.after(te, "s_ve")
            if do_q:
                proj_fm(0, 0, io["Qscr"], col0)

        for blk in range(16):
            proj_block(io["t_batch"], blk * 512, 4, do_q=False, do_kv=True, col0=blk * 512)
        for blk in range(17):
            proj_block(io["t_own"], blk * 256, 2, do_q=True, do_kv=False, col0=blk * 256)
        ctx.after(sync, "d_sy")
        ctx.after(gp, "d_sy")

    QG = 256
    NQG = TEXT // QG
    with (
        nc.sbuf_tensor("a_ksb", [128, 8 * 4096], BF16) as ksb,
        nc.sbuf_tensor("a_vsb", [128, 32 * 1024], BF16) as vsb,
        nc.sbuf_tensor("a_qsb", [128, 2 * 8 * QG], BF16) as qsb,
        nc.sbuf_tensor("a_pt", [128, 2 * QG], BF16) as ptb,
        nc.sbuf_tensor("a_oev", [128, 2 * 1024], F32) as oev,
        nc.sbuf_tensor("a_opr", [128, 1024], F32) as opr,
        nc.sbuf_tensor("a_res", [128, 1024], F32) as resid,
        nc.sbuf_tensor("a_rs", [128, 2128], F32) as rssb_pad,
        nc.sbuf_tensor("a_x1", [128, 1024], F32) as x1t,
        nc.sbuf_tensor("a_lncol", [128, 8], F32) as lncol,
        nc.sbuf_tensor("a_sq", [128, 1024], F32) as sqdump,
        nc.psum_tensor("a_ps_s", [128, 2 * 512], F32) as ps_s,
        nc.psum_tensor("a_ps_o", [128, 4 * 512], F32) as ps_o,
        nc.psum_tensor("a_ps_r", [128, 2 * 512], F32) as ps_r_,
    ):
        rssb = rssb_pad[:, 1024:1104]
        ps_r = ps_r_
        for kh in range(2):
            ctx.after(sync, "s_te")
            ctx.after(gp, "s_te")
            for dc in range(DC):
                ctx.inc(sync.dma_start(out=ksb[:, dc * 4096:(dc + 1) * 4096],
                                       in_=io["Kscr"][dc * 128:(dc + 1) * 128, kh * 4096:(kh + 1) * 4096]),
                        "d_sy", 16)
            for vt in range(32):
                ctx.inc(sync.dma_start(
                    out=vsb[:, vt * 1024:(vt + 1) * 1024],
                    in_=io["Vscr"][kh * 4096 + vt * 128: kh * 4096 + (vt + 1) * 128, :]),
                    "d_sy", 16)
            ctx.after(te, "d_sy", "d_gp")

            for qg in range(NQG):
                qb = qg % 2
                ctx.after(sync, "s_te")
                for dc in range(DC):
                    ctx.inc(sync.dma_start(out=qsb[:, (qb * 8 + dc) * QG:(qb * 8 + dc + 1) * QG],
                                           in_=io["Qscr"][dc * 128:(dc + 1) * 128, qg * QG:(qg + 1) * QG]),
                            "d_sy", 16)
                ctx.after(te, "d_sy")
                exp_done = [None] * 32
                qk_done = [None] * 32

                def emit_qk(kt):
                    sb = kt % 2
                    for dc in range(DC):
                        m = te.matmul(ps_s[:, sb * 512: sb * 512 + 256],
                                      ksb[:, dc * 4096 + kt * 128: dc * 4096 + (kt + 1) * 128],
                                      qsb[:, (qb * 8 + dc) * QG:(qb * 8 + dc + 1) * QG],
                                      start=(dc == 0), stop=(dc == DC - 1))
                    qk_done[kt] = ctx.inc(m, "s_te")

                emit_qk(0)
                for kt in range(32):
                    sb = kt % 2
                    sc.wait_ge(ctx.sem("s_te"), qk_done[kt])
                    a = sc.activation(out=ptb[:, sb * 256:(sb + 1) * 256],
                                      in_=ps_s[:, sb * 512: sb * 512 + 256],
                                      func=ACT.Exp, scale=consts_sb[:, C_INV32:C_INV32 + 1])
                    exp_done[kt] = ctx.inc(a, "s_sc")
                    if kt + 1 < 32:
                        if kt >= 1:
                            te.wait_ge(ctx.sem("s_sc"), exp_done[kt - 1])
                        emit_qk(kt + 1)
                    te.wait_ge(ctx.sem("s_sc"), exp_done[kt])
                    for qt in range(2):
                        for vh in range(2):
                            m = te.matmul(ps_o[:, (qt * 2 + vh) * 512:(qt * 2 + vh + 1) * 512],
                                          ptb[:, sb * 256 + qt * 128: sb * 256 + (qt + 1) * 128],
                                          vsb[:, kt * 1024 + vh * 512: kt * 1024 + (vh + 1) * 512],
                                          start=(kt == 0), stop=(kt == 31))
                        m2 = te.matmul(ps_r[:, qt * 512: qt * 512 + 1],
                                       ptb[:, sb * 256 + qt * 128: sb * 256 + (qt + 1) * 128],
                                       io["ones_kcol_sb"],
                                       start=(kt == 0), stop=(kt == 31))
                    ctx.inc(m2, "s_te")
                if HARD_SYNC:
                    ctx.hard_sync()
                ctx.after(ve, "s_te")
                if kh == 0:
                    c = ve.tensor_copy(oev[:, 0:2048], ps_o[:, 0:2048])
                    c = ve.tensor_copy(rssb[:, qg * 2: qg * 2 + 1], ps_r[:, 0:1])
                    c = ve.tensor_copy(rssb[:, 40 + qg: 41 + qg], ps_r[:, 512:513])
                    ctx.inc(c, "s_ve")
                    ctx.after(sync, "s_ve")
                    for qt in range(2):
                        ctx.inc(sync.dma_start(
                            out=io["Oscr"][(qg * 2 + qt) * 128:(qg * 2 + qt + 1) * 128, :],
                            in_=oev[:, qt * 1024:(qt + 1) * 1024]), "d_sy", 16)
                    ctx.after(te, "s_ve")
                else:
                    for qt in range(2):
                        row = (qg * 2 + qt) * 128
                        ctx.after(sync, "s_ve")
                        ctx.inc(sync.dma_start(out=opr[:], in_=io["Oscr"][row:row + 128, :]),
                                "d_sy", 16)
                        ctx.inc(sync.dma_start(out=resid[:], in_=io["t_own"][row:row + 128, :]),
                                "d_sy", 16)
                        ctx.after(ve, "d_sy")
                        c = ve.tensor_tensor(out=rssb[:, 76 + qt: 77 + qt],
                                             in0=(rssb[:, qg * 2: qg * 2 + 1] if qt == 0
                                                  else rssb[:, 40 + qg: 41 + qg]),
                                             in1=ps_r[:, qt * 512: qt * 512 + 1], op=AluOpType.add)
                        ctx.inc(c, "s_ve")
                        ve.wait_ge(ctx.sem("s_ve"), ctx.counts["s_ve" + ctx.suffix])
                        c = ve.reciprocal(rssb[:, 78 + qt: 79 + qt], rssb[:, 76 + qt: 77 + qt])
                        ctx.inc(c, "s_ve")
                        ve.wait_ge(ctx.sem("s_ve"), ctx.counts["s_ve" + ctx.suffix])
                        c = ve.tensor_tensor(out=oev[:, 0:1024],
                                             in0=ps_o[:, qt * 1024:(qt + 1) * 1024],
                                             in1=opr[:], op=AluOpType.add)
                        ctx.inc(c, "s_ve")
                        ve.wait_ge(ctx.sem("s_ve"), ctx.counts["s_ve" + ctx.suffix])
                        c = ve.scalar_tensor_tensor(out=x1t[:], in0=oev[:, 0:1024],
                                                    scalar=rssb[:, 78 + qt: 79 + qt],
                                                    in1=resid[:],
                                                    op0=AluOpType.mult, op1=AluOpType.add)
                        ctx.inc(c, "s_ve")
                        ln_epilogue(ctx, ve, sc, x1t[:], lncol, sqdump[:], consts_sb, x1t[:])
                        ctx.after(sync, "s_ve")
                        ctx.inc(sync.dma_start(out=io["x1"][row:row + 128, :], in_=x1t[:]),
                                "d_sy", 16)
                    ctx.after(te, "s_ve")


def finish(ctx):
    sync = ctx.eng["sync"]
    for s in list(ctx.counts):
        if (s.startswith("d_sy") or s.startswith("d_gp")) and ctx.counts[s]:
            sync.wait_ge(ctx.sems[s], ctx.counts[s])



def host_stage_a_inputs(inp, core):
    b, half = core // 2, core % 2
    t = inp["L0_attn_t"][b]
    start = half * TOWN
    t_own = np.zeros((TEXT, D), np.float32)
    lo = start - 128
    s0 = max(lo, 0)
    e0 = min(start + TOWN + 128, S)
    t_own[s0 - lo: e0 - lo] = t[s0:e0]
    Wq = inp["L0_attn_W_qkv"]
    b_qkv = inp["L0_attn_b_qkv"]
    return {
        "t_batch": np.ascontiguousarray(t),
        "t_own": t_own,
        "WqkvT": np.ascontiguousarray(Wq.T),
        "bqkv_cols": np.ascontiguousarray(b_qkv.reshape(24, 128).T),
        "bv_row": np.ascontiguousarray(b_qkv[2048:].reshape(1, 1024)),
        "consts": CONSTS_NP,
        "ident": np.eye(128, dtype=ml_dtypes.bfloat16),
        "identf": np.eye(128, dtype=np.float32),
        "onesf_row": np.ones((1, 128), np.float32),
        "ones_col": np.ones((1, 128), ml_dtypes.bfloat16),
        "ones_kcol": np.ones((128, 1), ml_dtypes.bfloat16),
    }


def declare_common(nc, io):
    consts = nc.dram_tensor("consts", [128, 16], F32, kind="ExternalInput").ap()
    ident = nc.dram_tensor("ident", [128, 128], BF16, kind="ExternalInput").ap()
    identf = nc.dram_tensor("identf", [128, 128], F32, kind="ExternalInput").ap()
    onesf_row = nc.dram_tensor("onesf_row", [1, 128], F32, kind="ExternalInput").ap()
    ones_col = nc.dram_tensor("ones_col", [1, 128], BF16, kind="ExternalInput").ap()
    ones_kcol = nc.dram_tensor("ones_kcol", [128, 1], BF16, kind="ExternalInput").ap()
    io.update(consts=consts, ident=ident, identf=identf, onesf_row=onesf_row,
              ones_col=ones_col, ones_kcol=ones_kcol)
    return io


def load_common(ctx, io):
    nc = ctx.nc
    sync = ctx.eng["sync"]
    cs = ctx._stack.enter_context(nc.sbuf_tensor("c_consts", [128, 16], F32))
    idn = ctx._stack.enter_context(nc.sbuf_tensor("c_ident", [128, 128], BF16))
    idnf = ctx._stack.enter_context(nc.sbuf_tensor("c_identf", [128, 128], F32))
    onfr = ctx._stack.enter_context(nc.sbuf_tensor("c_onesfr", [1, 128], F32))
    oc = ctx._stack.enter_context(nc.sbuf_tensor("c_ones", [1, 128], BF16))
    okc = ctx._stack.enter_context(nc.sbuf_tensor("c_oneskc", [128, 1], BF16))
    ctx.inc(sync.dma_start(out=cs[:], in_=io["consts"]), "d_sy", 16)
    ctx.inc(sync.dma_start(out=idn[:], in_=io["ident"]), "d_sy", 16)
    ctx.inc(sync.dma_start(out=idnf[:], in_=io["identf"]), "d_sy", 16)
    ctx.inc(sync.dma_start(out=onfr[:], in_=io["onesf_row"]), "d_sy", 16)
    ctx.inc(sync.dma_start(out=oc[:], in_=io["ones_col"]), "d_sy", 16)
    ctx.inc(sync.dma_start(out=okc[:], in_=io["ones_kcol"]), "d_sy", 16)
    io["consts_sb"] = cs
    io["ident_sb"] = idn[:]
    io["identf_sb"] = idnf[:]
    io["onesf_row_sb"] = onfr[:]
    io["ones_col_sb"] = oc[:]
    io["ones_kcol_sb"] = okc[:]
    for e in ("tensor", "vector", "scalar", "gpsimd"):
        ctx.after(ctx.eng[e], "d_sy")


def quant_rows(ctx, io, x_ap, consts_sb, qb, out_bf_ap, r_col_out):
    ve, sc = ctx.eng["vector"], ctx.eng["scalar"]
    c = ve.tensor_reduce(out=qb["m"][:, 0:1], in_=x_ap, axis=AX.X,
                         op=AluOpType.max, apply_absolute_value=True)
    ctx.inc(c, "s_ve")
    ve.wait_ge(ctx.sem("s_ve"), ctx.counts["s_ve" + ctx.suffix])
    c = ve.tensor_scalar(out=qb["m"][:, 1:2], in0=qb["m"][:, 0:1], scalar1=1e-5,
                         scalar2=None, op0=AluOpType.max)
    ctx.inc(c, "s_ve")
    ve.wait_ge(ctx.sem("s_ve"), ctx.counts["s_ve" + ctx.suffix])
    c = ve.reciprocal(qb["m"][:, 2:3], qb["m"][:, 1:2])
    ctx.inc(c, "s_ve")
    ve.wait_ge(ctx.sem("s_ve"), ctx.counts["s_ve" + ctx.suffix])
    c = ve.tensor_scalar(out=qb["m"][:, 3:4], in0=qb["m"][:, 2:3], scalar1=127.0,
                         scalar2=None, op0=AluOpType.mult)
    c = ve.tensor_scalar(out=r_col_out, in0=qb["m"][:, 1:2], scalar1=1.0 / 127.0,
                         scalar2=None, op0=AluOpType.mult)
    ctx.inc(c, "s_ve")
    ctx.after(sc, "s_ve")
    a = sc.activation(out=qb["ci"], in_=x_ap, func=ACT.Identity, scale=qb["m"][:, 3:4])
    ctx.inc(a, "s_sc")
    ctx.after(ve, "s_sc")
    c = ve.scalar_tensor_tensor(out=qb["f"], in0=x_ap, scalar=qb["m"][:, 3:4],
                                in1=qb["ci"], op0=AluOpType.mult, op1=AluOpType.subtract)
    ctx.inc(c, "s_ve")
    ctx.after(sc, "s_ve")
    a = sc.activation(out=qb["a1"], in_=qb["f"], func=ACT.Sign,
                      bias=consts_sb[:, C_NEGHALF:C_NEGHALF + 1], scale=1.0)
    ctx.inc(a, "s_sc")
    sc.wait_ge(ctx.sem("s_sc"), ctx.counts["s_sc" + ctx.suffix])
    a = sc.activation(out=qb["f"], in_=qb["f"], func=ACT.Sign,
                      bias=consts_sb[:, C_HALF:C_HALF + 1], scale=1.0)
    ctx.inc(a, "s_sc")
    ctx.after(ve, "s_sc")
    c = ve.tensor_tensor(out=qb["a1"], in0=qb["a1"], in1=qb["f"], op=AluOpType.add)
    c = ve.scalar_tensor_tensor(out=qb["f"], in0=qb["a1"], scalar=0.5, in1=qb["ci"],
                                op0=AluOpType.mult, op1=AluOpType.add)
    ctx.inc(c, "s_ve")
    ve.wait_ge(ctx.sem("s_ve"), ctx.counts["s_ve" + ctx.suffix])
    c = ve.tensor_scalar(out=out_bf_ap, in0=qb["f"], scalar1=qb["m"][:, 1:2],
                         scalar2=1.0 / 127.0, op0=AluOpType.mult, op1=AluOpType.mult)
    ctx.inc(c, "s_ve")

def emit_stage_ffn(ctx, io, pfx):
    from contextlib import ExitStack
    nc = ctx.nc
    sync, gp, ve, sc, te = (ctx.eng[k] for k in ("sync", "gpsimd", "vector", "scalar", "tensor"))
    consts_sb = io["consts_sb"]
    P = pfx
    NBLK = NT // 2

    stW = ExitStack()
    w1q = stW.enter_context(nc.sbuf_tensor(P + "_w1q", [128, 8 * 4096], BF16))
    scl = stW.enter_context(nc.sbuf_tensor(P + "_sc", [128, 8], F32))
    cw = stW.enter_context(nc.sbuf_tensor(P + "_cw", [128, 96], F32))
    cb = stW.enter_context(nc.sbuf_tensor(P + "_cb", [128, 32], F32))
    msk = stW.enter_context(nc.sbuf_tensor(P + "_msk", [128, 2], F32))
    acc = stW.enter_context(nc.sbuf_tensor(P + "_acc", [128, 40], F32))

    ctx.inc(sync.dma_start(out=cw[:], in_=io[P + "_conv_w_cols"]), "d_sy", 16)
    ctx.inc(sync.dma_start(out=cb[:], in_=io[P + "_conv_b_cols"]), "d_sy", 16)
    ctx.inc(sync.dma_start(out=msk[:, 0:1], in_=io["mask_lo"]), "d_sy", 16)
    ctx.inc(sync.dma_start(out=msk[:, 1:2], in_=io["mask_hi"]), "d_sy", 16)

    stQ = ExitStack()
    wtb = stQ.enter_context(nc.sbuf_tensor(P + "_wqb", [128, 2 * 1024], F32))
    wdump = stQ.enter_context(nc.sbuf_tensor(P + "_wqd", [128, 1024], BF16))
    wci = stQ.enter_context(nc.sbuf_tensor(P + "_wqi", [128, 1024], I32))
    wf = stQ.enter_context(nc.sbuf_tensor(P + "_wqf", [128, 1024], F32))
    wa1 = stQ.enter_context(nc.sbuf_tensor(P + "_wa1", [128, 1024], F32))
    wq_ev = stQ.enter_context(nc.sbuf_tensor(P + "_wev", [128, 2 * 1024], BF16))
    ps_w = stQ.enter_context(nc.psum_tensor(P + "_ps_w", [128, 512], F32))

    def wquant(src_slice, n_slices, mbar_col, ws_col, sink):
        for i in range(n_slices):
            pb = i % 2
            ctx.after(sync, "s_sc")
            ctx.inc(sync.dma_start(out=wtb[:, pb * 1024:(pb + 1) * 1024],
                                   in_=src_slice(i)), "d_sy", 16)
            ctx.after(sc, "d_sy")
            a = sc.activation(out=wdump[:], in_=wtb[:, pb * 1024:(pb + 1) * 1024],
                              func=ACT.Abs, accum_out=acc[:, i: i + 1])
            ctx.inc(a, "s_sc")
        ctx.after(ve, "s_sc")
        c = ve.tensor_reduce(out=acc[:, 36:37], in_=acc[:, 0:n_slices], axis=AX.X,
                             op=AluOpType.add)
        ctx.inc(c, "s_ve")
        ctx.after(te, "s_ve", "s_sc")
        m = te.matmul(ps_w[0:1, 0:1], acc[:, 36:37],
                      consts_sb[:, C_ONE:C_ONE + 1], start=True, stop=True)
        ctx.inc(m, "s_te")
        ctx.after(sc, "s_te")
        a = sc.activation(out=acc[0:1, 38:39], in_=ps_w[0:1, 0:1], func=ACT.Copy)
        ctx.inc(a, "s_sc")
        ctx.after(te, "s_sc")
        m = te.matmul(ps_w[:, 1:2], io["onesf_row_sb"], acc[0:1, 38:39],
                      start=True, stop=True)
        ctx.inc(m, "s_te")
        ctx.after(ve, "s_te")
        c = ve.tensor_copy(acc[:, 37:38], ps_w[:, 1:2])
        ctx.inc(c, "s_ve")
        ve.wait_ge(ctx.sem("s_ve"), ctx.counts["s_ve" + ctx.suffix])
        nelem = float(n_slices * 128 * 1024)
        c = ve.tensor_scalar(out=mbar_col, in0=acc[:, 37:38], scalar1=1.0 / nelem,
                             scalar2=1e-5, op0=AluOpType.mult, op1=AluOpType.max)
        ctx.inc(c, "s_ve")
        ve.wait_ge(ctx.sem("s_ve"), ctx.counts["s_ve" + ctx.suffix])
        c = ve.reciprocal(ws_col, mbar_col)
        ctx.inc(c, "s_ve")
        for i in range(n_slices):
            pb = i % 2
            ctx.after(sync, "s_ve", "s_sc")
            ctx.inc(sync.dma_start(out=wtb[:, pb * 1024:(pb + 1) * 1024],
                                   in_=src_slice(i)), "d_sy", 16)
            ctx.after(sc, "d_sy", "s_ve")
            wt = wtb[:, pb * 1024:(pb + 1) * 1024]
            a = sc.activation(out=wci[:], in_=wt, func=ACT.Identity, scale=ws_col)
            ctx.inc(a, "s_sc")
            ctx.after(ve, "s_sc")
            c = ve.scalar_tensor_tensor(out=wf[:], in0=wt, scalar=ws_col, in1=wci[:],
                                        op0=AluOpType.mult, op1=AluOpType.subtract)
            ctx.inc(c, "s_ve")
            ctx.after(sc, "s_ve")
            a = sc.activation(out=wa1[:], in_=wf[:], func=ACT.Sign,
                              bias=consts_sb[:, C_NEGHALF:C_NEGHALF + 1], scale=1.0)
            ctx.inc(a, "s_sc")
            sc.wait_ge(ctx.sem("s_sc"), ctx.counts["s_sc" + ctx.suffix])
            a = sc.activation(out=wf[:], in_=wf[:], func=ACT.Sign,
                              bias=consts_sb[:, C_HALF:C_HALF + 1], scale=1.0)
            ctx.inc(a, "s_sc")
            ctx.after(ve, "s_sc")
            c = ve.tensor_tensor(out=wa1[:], in0=wa1[:], in1=wf[:], op=AluOpType.add)
            c = ve.scalar_tensor_tensor(out=wf[:], in0=wa1[:], scalar=0.5, in1=wci[:],
                                        op0=AluOpType.mult, op1=AluOpType.add)
            c = ve.tensor_scalar(out=wq_ev[:, pb * 1024:(pb + 1) * 1024], in0=wf[:],
                                 scalar1=1.0, scalar2=-1.0,
                                 op0=AluOpType.min, op1=AluOpType.max)
            ctx.inc(c, "s_ve")
            sink(i, wq_ev[:, pb * 1024:(pb + 1) * 1024])

    def src_w1(i):
        dm, cs = i // 4, i % 4
        return io[P + "_W1T"][dm * 128:(dm + 1) * 128, cs * 1024:(cs + 1) * 1024]

    def src_w2(i):
        return io[P + "_W2T"][i * 128:(i + 1) * 128, :]

    def sink_w1(i, src):
        dm, cs = i // 4, i % 4
        c = ve.tensor_copy(w1q[:, dm * 4096 + cs * 1024: dm * 4096 + (cs + 1) * 1024], src)
        ctx.inc(c, "s_ve")

    def sink_w2(i, src):
        ctx.after(sync, "s_ve")
        ctx.inc(sync.dma_start(out=io[P + "_w2qS"][i * 128:(i + 1) * 128, :], in_=src),
                "d_sy", 16)

    wquant(src_w1, 32, scl[:, 0:1], scl[:, 1:2], sink_w1)
    wquant(src_w2, 32, scl[:, 2:3], scl[:, 3:4], sink_w2)
    ctx.after(ve, "d_sy", "s_ve")
    c = ve.tensor_scalar(out=cw[:], in0=cw[:], scalar1=scl[:, 0:1], scalar2=None,
                         op0=AluOpType.mult)
    ctx.inc(c, "s_ve")
    stQ.close()

    stX = ExitStack()
    xt = stX.enter_context(nc.sbuf_tensor(P + "_xt", [128, 2 * 1024], F32))
    xq = stX.enter_context(nc.sbuf_tensor(P + "_xq", [128, 1024], BF16))
    qm = stX.enter_context(nc.sbuf_tensor(P + "_qm", [128, 8], F32))
    qci = stX.enter_context(nc.sbuf_tensor(P + "_qci", [128, 1024], I32))
    qf = stX.enter_context(nc.sbuf_tensor(P + "_qf", [128, 1024], F32))
    qa1 = stX.enter_context(nc.sbuf_tensor(P + "_qa1", [128, 1024], F32))
    rx = stX.enter_context(nc.sbuf_tensor(P + "_rx", [128, 40], F32))
    tev = stX.enter_context(nc.sbuf_tensor(P + "_tev", [128, 1024], BF16))
    ps_t = stX.enter_context(nc.psum_tensor(P + "_ps_t", [128, 2 * 1024], BF16))
    qb = {"m": qm, "ci": qci[:], "f": qf[:], "a1": qa1[:]}
    for tl in range(NT):
        pb = tl % 2
        ctx.after(sync, "s_sc", "s_ve")
        ctx.inc(sync.dma_start(out=xt[:, pb * 1024:(pb + 1) * 1024],
                               in_=io[P + "_xin"][tl * 128:(tl + 1) * 128, :]),
                "d_sy", 16)
        ctx.after(ve, "d_sy")
        quant_rows(ctx, io, xt[:, pb * 1024:(pb + 1) * 1024], consts_sb,
                   qb, xq[:], rx[:, tl:tl + 1])
        if tl == 0 or tl == NT - 1:
            ve.wait_ge(ctx.sem("s_ve"), ctx.counts["s_ve" + ctx.suffix])
            c = ve.tensor_scalar(out=xq[:], in0=xq[:],
                                 scalar1=msk[:, 0:1] if tl == 0 else msk[:, 1:2],
                                 scalar2=None, op0=AluOpType.mult)
            ctx.inc(c, "s_ve")
        xt_mm = [None] * DC
        xt_ev = [None] * DC

        def emit_xtr(dc):
            hb = dc % 2
            m = te.transpose(ps_t[:, hb * 1024: hb * 1024 + 128],
                             xq[:, dc * 128:(dc + 1) * 128], io["ident_sb"])
            xt_mm[dc] = ctx.inc(m, "s_te")

        ctx.after(te, "s_ve", "s_sc")
        emit_xtr(0)
        for dc in range(DC):
            if dc + 1 < DC:
                if dc >= 1:
                    te.wait_ge(ctx.sem("s_sc"), xt_ev[dc - 1])
                emit_xtr(dc + 1)
            sc.wait_ge(ctx.sem("s_te"), xt_mm[dc])
            a = sc.activation(out=tev[:, dc * 128:(dc + 1) * 128],
                              in_=ps_t[:, (dc % 2) * 1024: (dc % 2) * 1024 + 128], func=ACT.Copy)
            xt_ev[dc] = ctx.inc(a, "s_sc")
        ctx.after(sync, "s_sc")
        for dc in range(DC):
            ctx.inc(sync.dma_start(
                out=io[P + "_xqT"][dc * 128:(dc + 1) * 128, tl * 128:(tl + 1) * 128],
                in_=tev[:, dc * 128:(dc + 1) * 128]), "d_sy", 16)
    stX.close()

    st1 = ExitStack()
    xqtb = st1.enter_context(nc.sbuf_tensor(P + "_xqTb", [128, 2 * 8 * 258], BF16))
    hTM = st1.enter_context(nc.sbuf_tensor(P + "_hTM", [128, 2 * 4096], F32))
    hqTM = st1.enter_context(nc.sbuf_tensor(P + "_hqTM", [128, 2 * 4096], BF16))
    cvb = st1.enter_context(nc.sbuf_tensor(P + "_cv", [128, 3 * 256], F32))
    slb = st1.enter_context(nc.sbuf_tensor(P + "_sl", [128, 3 * 256], F32))
    qm2 = st1.enter_context(nc.sbuf_tensor(P + "_qm2", [128, 8], F32))
    qci2 = st1.enter_context(nc.sbuf_tensor(P + "_qci2", [128, 4096], I32))
    qf2 = st1.enter_context(nc.sbuf_tensor(P + "_qf2", [128, 4096], F32))
    qa12 = st1.enter_context(nc.sbuf_tensor(P + "_qa12", [128, 4096], F32))
    rh = st1.enter_context(nc.sbuf_tensor(P + "_rh", [128, 2], F32))
    hqTe = st1.enter_context(nc.sbuf_tensor(P + "_hqTe", [128, 2 * 256], BF16))
    ps_h = st1.enter_context(nc.psum_tensor(P + "_ps_h", [128, 4 * 512], F32))
    ps_tt = st1.enter_context(nc.psum_tensor(P + "_ps_tt", [128, 2 * 512], F32))
    ps_tb = st1.enter_context(nc.psum_tensor(P + "_ps_tb", [128, 2 * 1024], BF16))
    qb2 = {"m": qm2, "ci": qci2[:], "f": qf2[:], "a1": qa12[:]}
    blk_lo = [max(blk * 256 - 1, 0) for blk in range(NBLK)]
    blk_hi = [min(blk * 256 + 257, TEXT) for blk in range(NBLK)]
    blk_nc = [blk_hi[b] - blk_lo[b] for b in range(NBLK)]
    blk_bs = [blk * 256 - blk_lo[blk] for blk in range(NBLK)]
    NG = NBLK * 32
    w1_done = [None] * NG
    conv_done = [None] * NG
    silu_done = [None] * NG
    tr_done = [None] * NG
    ev_done = [None] * NG
    load_done = [None] * NBLK
    hT_free = [None] * NBLK
    tr2_last = [None] * NBLK

    def emit_loads(blk):
        pb = blk % 2
        if blk >= 2:
            sync.wait_ge(ctx.sem("s_te"), w1_done[(blk - 2) * 32 + 31])
        for dc in range(DC):
            d = sync.dma_start(
                out=xqtb[:, (pb * 8 + dc) * 258:(pb * 8 + dc) * 258 + blk_nc[blk]],
                in_=io[P + "_xqT"][dc * 128:(dc + 1) * 128, blk_lo[blk]:blk_hi[blk]])
            ctx.inc(d, "d_sy", 16)
        load_done[blk] = ctx.val("d_sy")

    def MM(g):
        blk, hc = g // 32, g % 32
        pb = blk % 2
        bank = g % 4
        if hc == 0:
            te.wait_ge(ctx.sem("d_sy"), load_done[blk])
        if g >= 4:
            te.wait_ge(ctx.sem("s_ve"), conv_done[g - 4])
        for dc in range(DC):
            m = te.matmul(ps_h[:, bank * 512: bank * 512 + blk_nc[blk]],
                          w1q[:, dc * 4096 + hc * 128: dc * 4096 + (hc + 1) * 128],
                          xqtb[:, (pb * 8 + dc) * 258:(pb * 8 + dc) * 258 + blk_nc[blk]],
                          start=(dc == 0), stop=(dc == DC - 1))
        w1_done[g] = ctx.inc(m, "s_te")

    def CONV(g):
        blk, hc = g // 32, g % 32
        bank, slot = g % 4, g % 3
        base, ncols = blk_bs[blk], blk_nc[blk]
        ve.wait_ge(ctx.sem("s_te"), w1_done[g])
        if g >= 3:
            ve.wait_ge(ctx.sem("s_sc"), silu_done[g - 3])
        cv = cvb[:, slot * 256: slot * 256 + 256]
        c = ve.tensor_scalar(out=cv,
                             in0=ps_h[:, bank * 512 + base: bank * 512 + base + 256],
                             scalar1=cw[:, 3 * hc + 1: 3 * hc + 2], scalar2=None,
                             op0=AluOpType.mult)
        if base >= 1:
            c = ve.scalar_tensor_tensor(
                out=cv,
                in0=ps_h[:, bank * 512 + base - 1: bank * 512 + base + 255],
                scalar=cw[:, 3 * hc: 3 * hc + 1], in1=cv,
                op0=AluOpType.mult, op1=AluOpType.add)
        else:
            c = ve.scalar_tensor_tensor(
                out=cvb[:, slot * 256 + 1: slot * 256 + 256],
                in0=ps_h[:, bank * 512: bank * 512 + 255],
                scalar=cw[:, 3 * hc: 3 * hc + 1],
                in1=cvb[:, slot * 256 + 1: slot * 256 + 256],
                op0=AluOpType.mult, op1=AluOpType.add)
        w2w = min(256, ncols - base - 1)
        c = ve.scalar_tensor_tensor(
            out=cvb[:, slot * 256: slot * 256 + w2w],
            in0=ps_h[:, bank * 512 + base + 1: bank * 512 + base + 1 + w2w],
            scalar=cw[:, 3 * hc + 2: 3 * hc + 3],
            in1=cvb[:, slot * 256: slot * 256 + w2w],
            op0=AluOpType.mult, op1=AluOpType.add)
        conv_done[g] = ctx.inc(c, "s_ve")

    def SILU(g):
        blk, hc = g // 32, g % 32
        slot = g % 3
        sc.wait_ge(ctx.sem("s_ve"), conv_done[g])
        if g >= 3:
            sc.wait_ge(ctx.sem("s_te"), tr_done[g - 3])
        a = sc.activation(out=slb[:, slot * 256: slot * 256 + 256],
                          in_=cvb[:, slot * 256: slot * 256 + 256], func=ACT.Silu,
                          bias=cb[:, hc:hc + 1], scale=1.0)
        silu_done[g] = ctx.inc(a, "s_sc")

    def TR(g):
        slot, b2 = g % 3, g % 2
        te.wait_ge(ctx.sem("s_sc"), ev_done[g - 2] if g >= 2 else silu_done[g])
        for i in range(2):
            m = te.transpose(ps_tt[:, b2 * 512 + i * 128: b2 * 512 + (i + 1) * 128],
                             slb[:, slot * 256 + i * 128: slot * 256 + (i + 1) * 128],
                             io["identf_sb"])
        tr_done[g] = ctx.inc(m, "s_te")

    def EVAC(g):
        blk, hc = g // 32, g % 32
        b2 = g % 2
        sc.wait_ge(ctx.sem("s_te"), tr_done[g])
        if hc == 0 and blk >= 1:
            sc.wait_ge(ctx.sem("s_ve"), hT_free[blk - 1])
        for i in range(2):
            a = sc.activation(out=hTM[:, i * 4096 + hc * 128: i * 4096 + (hc + 1) * 128],
                              in_=ps_tt[:, b2 * 512 + i * 128: b2 * 512 + i * 128 + 128],
                              func=ACT.Copy)
        ev_done[g] = ctx.inc(a, "s_sc")

    def TAIL(blk):
        for i in range(2):
            if i == 0:
                ve.wait_ge(ctx.sem("s_sc"), ev_done[blk * 32 + 31])
                if blk >= 1:
                    ve.wait_ge(ctx.sem("s_te"), tr2_last[blk - 1])
            quant_rows(ctx, io, hTM[:, i * 4096:(i + 1) * 4096], consts_sb,
                       qb2, hqTM[:, i * 4096:(i + 1) * 4096], rh[:, i:i + 1])
        hT_free[blk] = ctx.val("s_ve")
        qdone = ctx.val("s_ve")
        t_mm = [None] * 64
        t_ev = [None] * 64
        t_dma = [None] * 64

        def TR2(k):
            i, hc = k // 32, k % 32
            hb = k % 2
            if k == 0:
                te.wait_ge(ctx.sem("s_ve"), qdone)
            if k >= 2:
                te.wait_ge(ctx.sem("s_sc"), t_ev[k - 2])
            m = te.transpose(ps_tb[:, hb * 1024: hb * 1024 + 128],
                             hqTM[:, i * 4096 + hc * 128: i * 4096 + (hc + 1) * 128],
                             io["ident_sb"])
            t_mm[k] = ctx.inc(m, "s_te")

        def EV2(k):
            hb = k % 2
            sc.wait_ge(ctx.sem("s_te"), t_mm[k])
            if k >= 2:
                sc.wait_ge(ctx.sem("d_sy"), t_dma[k - 2])
            a = sc.activation(out=hqTe[:, hb * 256: hb * 256 + 128],
                              in_=ps_tb[:, hb * 1024: hb * 1024 + 128], func=ACT.Copy)
            t_ev[k] = ctx.inc(a, "s_sc")

        def DMA2(k):
            i, hc = k // 32, k % 32
            hb = k % 2
            tl = blk * 2 + i
            sync.wait_ge(ctx.sem("s_sc"), t_ev[k])
            t_dma[k] = ctx.inc(sync.dma_start(
                out=io[P + "_hqT"][hc * 128:(hc + 1) * 128, tl * 128:(tl + 1) * 128],
                in_=hqTe[:, hb * 256: hb * 256 + 128]), "d_sy", 16)

        for s2 in range(64 + 3):
            if 0 <= s2 - 2 < 64:
                EV2(s2 - 2)
            if 0 <= s2 - 3 < 64:
                DMA2(s2 - 3)
            if s2 < 64:
                TR2(s2)
        tr2_last[blk] = t_mm[63]

    ctx.after(te, "s_ve", "s_sc")
    ctx.after(sc, "d_sy")
    emit_loads(0)
    for s in range(NG + 5):
        if s < NG:
            MM(s)
        if s >= 8 and (s - 8) % 32 == 0 and (s - 8) // 32 + 1 < NBLK:
            emit_loads((s - 8) // 32 + 1)
        if 0 <= s - 2 < NG:
            CONV(s - 2)
        if 0 <= s - 3 < NG:
            SILU(s - 3)
        if 0 <= s - 4 < NG:
            TR(s - 4)
        if 0 <= s - 5 < NG:
            EVAC(s - 5)
        if s >= 36 and (s - 36) % 32 == 0 and (s - 36) // 32 < NBLK:
            TAIL((s - 36) // 32)
    st1.close()

    st2 = ExitStack()
    w2q = st2.enter_context(nc.sbuf_tensor(P + "_w2q", [128, 32 * 1024], BF16))
    hqs = st2.enter_context(nc.sbuf_tensor(P + "_hqs", [128, 2 * 32 * 128], BF16))
    resb = st2.enter_context(nc.sbuf_tensor(P + "_res", [128, 1024], F32))
    x2t = st2.enter_context(nc.sbuf_tensor(P + "_x2t", [128, 1024], F32))
    lncol = st2.enter_context(nc.sbuf_tensor(P + "_ln", [128, 8], F32))
    sqd = st2.enter_context(nc.sbuf_tensor(P + "_sq", [128, 1024], F32))
    ps_o = st2.enter_context(nc.psum_tensor(P + "_ps_o", [128, 4 * 512], F32))
    xbf_t = st2.enter_context(nc.sbuf_tensor(P + "_xbf", [128, 1024], BF16))
    ctx.after(gp, "d_sy")
    for i in range(32):
        ctx.inc(gp.dma_start(out=w2q[:, i * 1024:(i + 1) * 1024],
                             in_=io[P + "_w2qS"][i * 128:(i + 1) * 128, :]), "d_gp", 16)
    ctx.after(te, "d_gp")
    dma_done_t = {}
    mm_done_t = {}

    def emit_hqs_load(tl):
        pb = tl % 2
        for hc in range(32):
            d = sync.dma_start(
                out=hqs[:, (pb * 32 + hc) * 128:(pb * 32 + hc + 1) * 128],
                in_=io[P + "_hqT"][hc * 128:(hc + 1) * 128, tl * 128:(tl + 1) * 128])
            ctx.inc(d, "d_sy", 16)
        dma_done_t[tl] = ctx.val("d_sy")

    emit_hqs_load(1)
    for tl in range(1, 33):
        pb = tl % 2
        if tl + 1 < 33:
            if tl >= 2:
                sync.wait_ge(ctx.sem("s_te"), mm_done_t[tl - 1])
            emit_hqs_load(tl + 1)
        te.wait_ge(ctx.sem("d_sy"), dma_done_t[tl])
        for hc in range(32):
            for vh in range(2):
                m = te.matmul(ps_o[:, (pb * 2 + vh) * 512:(pb * 2 + vh + 1) * 512],
                              hqs[:, (pb * 32 + hc) * 128:(pb * 32 + hc + 1) * 128],
                              w2q[:, hc * 1024 + vh * 512: hc * 1024 + (vh + 1) * 512],
                              start=(hc == 0), stop=(hc == 31))
        mm_done_t[tl] = ctx.inc(m, "s_te")
        ctx.after(sync, "s_ve")
        ctx.inc(sync.dma_start(out=resb[:], in_=io[P + "_xin"][tl * 128:(tl + 1) * 128, :]),
                "d_sy", 16)
        ctx.after(ve, "d_sy", "s_te")
        for vh in range(2):
            c = ve.scalar_tensor_tensor(
                out=x2t[:, vh * 512:(vh + 1) * 512],
                in0=ps_o[:, (pb * 2 + vh) * 512:(pb * 2 + vh + 1) * 512],
                scalar=scl[:, 2:3], in1=resb[:, vh * 512:(vh + 1) * 512],
                op0=AluOpType.mult, op1=AluOpType.add)
        ctx.inc(c, "s_ve")
        ln_epilogue(ctx, ve, sc, x2t[:], lncol, sqd[:], consts_sb, x2t[:])
        ctx.after(sync, "s_ve")
        ctx.inc(sync.dma_start(out=io[P + "_xout"][tl * 128:(tl + 1) * 128, :],
                               in_=x2t[:]), "d_sy", 16)
        if P + "_xout_bf" in io:
            ctx.after(sc, "s_ve", "d_sy")
            a = sc.activation(out=xbf_t[:], in_=x2t[:], func=ACT.Copy)
            ctx.inc(a, "s_sc")
            ctx.after(sync, "s_sc")
            ctx.inc(sync.dma_start(out=io[P + "_xout_bf"][(tl - 1) * 128: tl * 128, :],
                                   in_=xbf_t[:]), "d_sy", 16)
        ctx.after(ve, "d_sy")
        ctx.after(te, "s_ve")
    st2.close()
    stW.close()




def host_ext_slice(full_b, core):
    half = core % 2
    start = half * TOWN
    out = np.zeros((TEXT, full_b.shape[1]), np.float32)
    lo = start - 128
    s0, e0 = max(lo, 0), min(start + TOWN + 128, S)
    out[s0 - lo: e0 - lo] = full_b[s0:e0]
    return out


def host_stage_ffn_inputs(inp, core, lkey, pfx):
    half = core % 2
    w1 = inp[lkey + "_w1"]
    w2 = inp[lkey + "_w2"]
    cwv = inp[lkey + "_conv_w"][:, 0, :]
    cbv = inp[lkey + "_conv_b"]
    return {
        pfx + "_W1T": np.ascontiguousarray(w1.T),
        pfx + "_W2T": np.ascontiguousarray(w2.T),
        pfx + "_conv_w_cols": np.ascontiguousarray(
            cwv.reshape(32, 128, 3).transpose(1, 0, 2).reshape(128, 96)),
        pfx + "_conv_b_cols": np.ascontiguousarray(cbv.reshape(32, 128).T),
        "mask_lo": np.full((128, 1), 0.0 if half == 0 else 1.0, np.float32),
        "mask_hi": np.full((128, 1), 0.0 if half == 1 else 1.0, np.float32),
    }


DG_C = 8
TC1 = 512
TC2 = 256
NCH1 = S // TC1
NCH2 = S // TC2
SEG = TC2 + 1


def emit_stage_c1(ctx, io):
    from contextlib import ExitStack
    nc = ctx.nc
    sync, gp, ve, sc, te = (ctx.eng[k] for k in ("sync", "gpsimd", "vector", "scalar", "tensor"))
    consts_sb = io["consts_sb"]
    st = ExitStack()
    win = st.enter_context(nc.sbuf_tensor("c_win", [128, 8 * 2048], BF16))
    wx = st.enter_context(nc.sbuf_tensor("c_wx", [128, 8 * 33], BF16))
    ccw = st.enter_context(nc.sbuf_tensor("c_ccw", [128, 32], F32))
    ccb = st.enter_context(nc.sbuf_tensor("c_ccb", [128, 8], F32))
    xt = st.enter_context(nc.sbuf_tensor("c_xt", [128, 4 * 1024], BF16))
    xT = st.enter_context(nc.sbuf_tensor("c_xT", [128, 8 * 512], BF16))
    xib = st.enter_context(nc.sbuf_tensor("c_xib", [128, 8 * 515], F32))
    ub = st.enter_context(nc.sbuf_tensor("c_ub", [128, 8 * 512], F32))
    ubb = st.enter_context(nc.sbuf_tensor("c_ubb", [128, 8 * 512], BF16))
    zb = st.enter_context(nc.sbuf_tensor("c_zb", [128, 8 * 512], BF16))
    cvt = st.enter_context(nc.sbuf_tensor("c_cvt", [128, 512], F32))
    xde = st.enter_context(nc.sbuf_tensor("c_xde", [128, 512], F32))
    ps_t = st.enter_context(nc.psum_tensor("c_ps_t", [128, 2 * 1024], BF16))
    ps_p = st.enter_context(nc.psum_tensor("c_ps_p", [128, 2 * 512], F32))
    ps_x = st.enter_context(nc.psum_tensor("c_ps_x", [128, 512], F32))

    for dc in range(DC):
        ctx.inc(gp.dma_start(out=win[:, dc * 2048:(dc + 1) * 2048],
                             in_=io["c_WinT"][dc * 128:(dc + 1) * 128, :]), "d_gp", 16)
    ctx.inc(gp.dma_start(out=wx[:], in_=bass.AP(io["c_WxT"].tensor, 0,
                                                [[33, 128], [33 * 128, 8], [1, 33]])),
            "d_gp", 16)
    ctx.inc(sync.dma_start(out=ccw[:], in_=io["c_conv_w_cols"]), "d_sy", 16)
    ctx.inc(sync.dma_start(out=ccb[:], in_=io["c_conv_b_cols"]), "d_sy", 16)
    ctx.after(ve, "d_sy")
    c = ve.memset(xib[:], 0.0)
    ctx.inc(c, "s_ve")
    ctx.after(te, "d_gp", "d_sy")
    ctx.after(sc, "d_gp", "d_sy")

    for ch in range(NCH1):
        t0 = ch * TC1
        ctx.after(sync, "s_te")
        for i in range(4):
            ctx.inc(sync.dma_start(out=xt[:, i * 1024:(i + 1) * 1024],
                                   in_=io["c_x2bf"][t0 + i * 128: t0 + (i + 1) * 128, :]),
                    "d_sy", 16)
        ctx.after(te, "d_sy")
        for dc in range(DC):
            hb = dc % 2
            ctx.after(te, "s_ve")
            for i in range(4):
                m = te.transpose(ps_t[:, hb * 1024 + i * 128: hb * 1024 + (i + 1) * 128],
                                 xt[:, i * 1024 + dc * 128: i * 1024 + (dc + 1) * 128],
                                 io["ident_sb"])
            ctx.inc(m, "s_te")
            ctx.after(ve, "s_te")
            c = ve.tensor_copy(xT[:, dc * 512:(dc + 1) * 512], ps_t[:, hb * 1024: hb * 1024 + 512])
            ctx.inc(c, "s_ve")
        ctx.after(te, "s_ve")
        for fc in range(16):
            hb = fc % 2
            ctx.after(te, "s_ve", "s_sc")
            for dc in range(DC):
                m = te.matmul(ps_p[:, hb * 512:(hb + 1) * 512],
                              win[:, dc * 2048 + fc * 128: dc * 2048 + (fc + 1) * 128],
                              xT[:, dc * 512:(dc + 1) * 512],
                              start=(dc == 0), stop=(dc == DC - 1))
            ctx.inc(m, "s_te")
            if fc < 8:
                dg = fc
                ctx.after(ve, "s_te")
                c = ve.tensor_copy(xib[:, dg * 515 + 3: dg * 515 + 515],
                                   ps_p[:, hb * 512:(hb + 1) * 512])
                ctx.inc(c, "s_ve")
            else:
                dg = fc - 8
                ctx.after(sc, "s_te")
                a = sc.activation(out=zb[:, dg * 512:(dg + 1) * 512],
                                  in_=ps_p[:, hb * 512:(hb + 1) * 512], func=ACT.Copy)
                ctx.inc(a, "s_sc")
        if "c_xib_dbg" in io and ch == 1:
            ctx.after(sync, "s_ve", "s_sc")
            ctx.inc(sync.dma_start(out=io["c_xib_dbg"], in_=xib[:]), "d_sy", 16)
        for dg in range(DG_C):
            ctx.after(ve, "s_sc")
            c = ve.tensor_scalar(out=cvt[:], in0=xib[:, dg * 515 + 3: dg * 515 + 515],
                                 scalar1=ccw[:, dg * 4 + 3: dg * 4 + 4], scalar2=None,
                                 op0=AluOpType.mult)
            for j in range(3):
                c = ve.scalar_tensor_tensor(
                    out=cvt[:], in0=xib[:, dg * 515 + j: dg * 515 + j + 512],
                    scalar=ccw[:, dg * 4 + j: dg * 4 + j + 1], in1=cvt[:],
                    op0=AluOpType.mult, op1=AluOpType.add)
            ctx.inc(c, "s_ve")
            ctx.after(sc, "s_ve", "d_sy")
            a = sc.activation(out=ub[:, dg * 512:(dg + 1) * 512], in_=cvt[:],
                              func=ACT.Silu, bias=ccb[:, dg:dg + 1], scale=1.0)
            ctx.inc(a, "s_sc")
        c = ve.tensor_copy(bass.AP(xib.ap().tensor, 0, [[8 * 515, 128], [515, 8], [1, 3]]),
                           bass.AP(xib.ap().tensor, 512, [[8 * 515, 128], [515, 8], [1, 3]]))
        ctx.inc(c, "s_ve")
        ctx.after(sc, "s_sc")
        a = sc.activation(out=ubb[:], in_=ub[:], func=ACT.Copy)
        ctx.inc(a, "s_sc")
        ctx.after(te, "s_sc")
        for dg in range(DG_C):
            m = te.matmul(ps_x[0:33, 0:512], wx[:, dg * 33:(dg + 1) * 33],
                          ubb[:, dg * 512:(dg + 1) * 512],
                          start=(dg == 0), stop=(dg == DG_C - 1))
        ctx.inc(m, "s_te")
        ctx.after(ve, "s_te")
        c = ve.tensor_copy(xde[0:33, :], ps_x[0:33, 0:512])
        ctx.inc(c, "s_ve")
        ctx.after(sync, "s_sc", "s_ve")
        for dg in range(DG_C):
            ctx.inc(sync.dma_start(out=io["c_u"][dg * 128:(dg + 1) * 128, t0:t0 + TC1],
                                   in_=ub[:, dg * 512:(dg + 1) * 512]), "d_sy", 16)
            ctx.inc(sync.dma_start(out=io["c_z"][dg * 128:(dg + 1) * 128, t0:t0 + TC1],
                                   in_=zb[:, dg * 512:(dg + 1) * 512]), "d_sy", 16)
        ctx.inc(sync.dma_start(out=io["c_xdbl"][0:33, t0:t0 + TC1], in_=xde[0:33, :]),
                "d_sy", 16)
        ctx.after(te, "s_ve")
    st.close()


def emit_stage_c2(ctx, io):
    from contextlib import ExitStack
    nc = ctx.nc
    sync, gp, ve, sc, te = (ctx.eng[k] for k in ("sync", "gpsimd", "vector", "scalar", "tensor"))
    consts_sb = io["consts_sb"]
    st = ExitStack()
    wout = st.enter_context(nc.sbuf_tensor("c_wout", [128, 8 * 1024], BF16))
    acols = st.enter_context(nc.sbuf_tensor("c_acols", [128, 8 * 16], F32))
    dtc = st.enter_context(nc.sbuf_tensor("c_dtc", [128, 16], F32))
    ub2 = st.enter_context(nc.sbuf_tensor("c_ub2", [128, 2 * 8 * TC2], F32))
    zb2 = st.enter_context(nc.sbuf_tensor("c_zb2", [128, 2 * 8 * TC2], BF16))
    xdrows = st.enter_context(nc.sbuf_tensor("c_xdrows", [1, 33 * TC2], BF16))
    dlt = st.enter_context(nc.sbuf_tensor("c_dlt", [128, 2 * 8 * TC2], F32))
    dub = st.enter_context(nc.sbuf_tensor("c_dub", [128, TC2], F32))
    brep = st.enter_context(nc.sbuf_tensor("c_brep", [128, 16 * SEG], BF16))
    crep = st.enter_context(nc.sbuf_tensor("c_crep", [128, 16 * SEG], BF16))
    abuf = st.enter_context(nc.sbuf_tensor("c_abuf", [128, 2 * 16 * SEG], F32))
    bbuf = st.enter_context(nc.sbuf_tensor("c_bbuf", [128, 16 * SEG], F32))
    hbuf = st.enter_context(nc.sbuf_tensor("c_hbuf", [128, 16 * SEG], F32))
    stc = st.enter_context(nc.sbuf_tensor("c_stc", [128, 8 * 16], F32))
    yb = st.enter_context(nc.sbuf_tensor("c_yb", [128, 8 * TC2], F32))
    spw = st.enter_context(nc.sbuf_tensor("c_spw", [128, TC2], F32))
    ygb = st.enter_context(nc.sbuf_tensor("c_ygb", [128, 8 * TC2], BF16))
    oev = st.enter_context(nc.sbuf_tensor("c_oev", [128, 2 * 1024], BF16))
    ps_r = st.enter_context(nc.psum_tensor("c_ps_r", [128, 2 * 512], F32))
    ps_d = st.enter_context(nc.psum_tensor("c_ps_d", [128, 512], F32))
    ps_o = st.enter_context(nc.psum_tensor("c_ps_o", [128, 4 * 512], F32))

    for dg in range(DG_C):
        ctx.inc(gp.dma_start(out=wout[:, dg * 1024:(dg + 1) * 1024],
                             in_=io["c_WoutT"][dg * 128:(dg + 1) * 128, :]), "d_gp", 16)
    ctx.inc(sync.dma_start(out=acols[:], in_=io["c_Acols"]), "d_sy", 16)
    ctx.inc(sync.dma_start(out=dtc[:], in_=io["c_dt_cols"]), "d_sy", 16)
    ctx.after(ve, "d_sy")
    c = ve.memset(abuf[:], 0.0)
    c = ve.memset(stc[:], 0.0)
    c = ve.memset(bbuf[:], 0.0)
    c = ve.memset(crep[:], 0.0)
    ctx.inc(c, "s_ve")
    ctx.after(te, "d_gp", "d_sy")
    ctx.after(sc, "d_sy", "s_ve")

    NCH = NCH2
    load_done = [None] * NCH
    xd_done = [None] * NCH
    dpre_mm = [None] * NCH
    dlt_done = [None] * NCH
    rep_ev_last = [None] * NCH
    rep_mm_last = [None] * NCH
    dg_end = [None] * NCH
    chunk_ve_end = [None] * NCH
    oev_dma = [None] * NCH
    oev_ev = [None] * NCH
    wout_done = [None] * NCH

    def HEAD(ch):
        pc = ch % 2
        t0 = ch * TC2
        if ch >= 2:
            sync.wait_ge(ctx.sem("s_ve"), chunk_ve_end[ch - 2])
        for dg in range(DG_C):
            ctx.inc(sync.dma_start(
                out=ub2[:, (pc * 8 + dg) * TC2:(pc * 8 + dg + 1) * TC2],
                in_=io["c_u"][dg * 128:(dg + 1) * 128, t0:t0 + TC2]), "d_sy", 16)
            ctx.inc(sync.dma_start(
                out=zb2[:, (pc * 8 + dg) * TC2:(pc * 8 + dg + 1) * TC2],
                in_=io["c_z"][dg * 128:(dg + 1) * 128, t0:t0 + TC2]), "d_sy", 16)
        load_done[ch] = ctx.val("d_sy")
        if ch >= 1:
            gp.wait_ge(ctx.sem("s_te"), rep_mm_last[ch - 1])
        ctx.inc(gp.dma_start(out=xdrows[:],
                             in_=bass.AP(io["c_xdbl_sum"].tensor, t0,
                                         [[1, 1], [S, 33], [1, TC2]])), "d_gp", 16)
        xd_done[ch] = ctx.val("d_gp")
        te.wait_ge(ctx.sem("d_gp"), xd_done[ch])
        if ch >= 1:
            te.wait_ge(ctx.sem("s_sc"), dlt_done[ch - 1])
            te.wait_ge(ctx.sem("s_sc"), rep_ev_last[ch - 1])
        m = te.matmul(ps_d[:, 0:TC2], io["ones_col_sb"], xdrows[0:1, 0:TC2],
                      start=True, stop=True)
        dpre_mm[ch] = ctx.inc(m, "s_te")
        sc.wait_ge(ctx.sem("s_te"), dpre_mm[ch])
        for dg in range(DG_C):
            a = sc.activation(out=spw[:, 0:TC2], in_=ps_d[:, 0:TC2],
                              func=ACT.Exp, scale=dtc[:, dg:dg + 1],
                              bias=dtc[:, 8 + dg: 9 + dg])
            ctx.inc(a, "s_sc")
            sc.wait_ge(ctx.sem("s_sc"), ctx.counts["s_sc" + ctx.suffix])
            a = sc.activation(out=dlt[:, (pc * 8 + dg) * TC2:(pc * 8 + dg + 1) * TC2],
                              in_=spw[:, 0:TC2], func=ACT.Ln, scale=1.0,
                              bias=consts_sb[:, C_ONE:C_ONE + 1])
            ctx.inc(a, "s_sc")
        dlt_done[ch] = ctx.val("s_sc")
        r_mm = [None] * 32
        r_ev = [None] * 32

        def emit_rep(n):
            hb = n % 2
            m = te.matmul(ps_r[:, hb * 512: hb * 512 + TC2], io["ones_col_sb"],
                          xdrows[0:1, (1 + n) * TC2:(2 + n) * TC2], start=True, stop=True)
            r_mm[n] = ctx.inc(m, "s_te")

        emit_rep(0)
        for n in range(32):
            if n + 1 < 32:
                if n >= 1:
                    te.wait_ge(ctx.sem("s_sc"), r_ev[n - 1])
                emit_rep(n + 1)
            sc.wait_ge(ctx.sem("s_te"), r_mm[n])
            if n == 0 and ch >= 1:
                sc.wait_ge(ctx.sem("s_ve"), dg_end[ch - 1])
            dst = brep if n < 16 else crep
            seg = n if n < 16 else n - 16
            a = sc.activation(out=dst[:, seg * SEG + 1: seg * SEG + 1 + TC2],
                              in_=ps_r[:, (n % 2) * 512: (n % 2) * 512 + TC2], func=ACT.Copy)
            r_ev[n] = ctx.inc(a, "s_sc")
        rep_ev_last[ch] = r_ev[31]
        rep_mm_last[ch] = r_mm[31]

    def DGLOOP(ch):
        pc = ch % 2
        a_done = [None] * DG_C
        ve_done = [None] * DG_C

        def emit_abuild(dg):
            par = dg % 2
            for n in range(16):
                a = sc.activation(out=abuf[:, par * 16 * SEG + n * SEG + 1:
                                           par * 16 * SEG + n * SEG + 1 + TC2],
                                  in_=dlt[:, (pc * 8 + dg) * TC2:(pc * 8 + dg + 1) * TC2],
                                  func=ACT.Exp,
                                  scale=acols[:, dg * 16 + n: dg * 16 + n + 1])
            a_done[dg] = ctx.inc(a, "s_sc")

        ctx.after(sc, "s_ve")
        emit_abuild(0)
        for dg in range(DG_C):
            par = dg % 2
            ab = par * 16 * SEG
            if dg + 1 < DG_C:
                if dg >= 1:
                    sc.wait_ge(ctx.sem("s_ve"), ve_done[dg - 1])
                emit_abuild(dg + 1)
            if dg == 0:
                ve.wait_ge(ctx.sem("d_sy"), load_done[ch])
                ve.wait_ge(ctx.sem("s_sc"), rep_ev_last[ch])
            c = ve.tensor_tensor(out=dub[:, 0:TC2],
                                 in0=dlt[:, (pc * 8 + dg) * TC2:(pc * 8 + dg + 1) * TC2],
                                 in1=ub2[:, (pc * 8 + dg) * TC2:(pc * 8 + dg + 1) * TC2],
                                 op=AluOpType.mult)
            for n in range(16):
                c = ve.tensor_tensor(out=bbuf[:, n * SEG + 1: n * SEG + 1 + TC2],
                                     in0=dub[:, 0:TC2],
                                     in1=brep[:, n * SEG + 1: n * SEG + 1 + TC2],
                                     op=AluOpType.mult)
            c = ve.tensor_copy(bass.AP(bbuf.ap().tensor, 0, [[16 * SEG, 128], [SEG, 16]]),
                               stc[:, dg * 16:(dg + 1) * 16])
            ctx.inc(c, "s_ve")
            ve.wait_ge(ctx.sem("s_ve"), ctx.counts["s_ve" + ctx.suffix])
            ve.wait_ge(ctx.sem("s_sc"), a_done[dg])
            c = ve.tensor_tensor_scan(out=hbuf[:],
                                      data0=abuf[:, ab: ab + 16 * SEG], data1=bbuf[:],
                                      initial=0.0, op0=AluOpType.mult, op1=AluOpType.add)
            ctx.inc(c, "s_ve")
            ve.wait_ge(ctx.sem("s_ve"), ctx.counts["s_ve" + ctx.suffix])
            c = ve.tensor_copy(stc[:, dg * 16:(dg + 1) * 16],
                               bass.AP(hbuf.ap().tensor, TC2, [[16 * SEG, 128], [SEG, 16]]))
            ctx.inc(c, "s_ve")
            c = ve.tensor_tensor(out=bbuf[:], in0=hbuf[:], in1=crep[:],
                                 op=AluOpType.mult)
            c = ve.tensor_tensor(out=bbuf[:, 0:8 * SEG], in0=bbuf[:, 0:8 * SEG],
                                 in1=bbuf[:, 8 * SEG:16 * SEG], op=AluOpType.add)
            c = ve.tensor_tensor(out=bbuf[:, 0:4 * SEG], in0=bbuf[:, 0:4 * SEG],
                                 in1=bbuf[:, 4 * SEG:8 * SEG], op=AluOpType.add)
            c = ve.tensor_tensor(out=bbuf[:, 0:2 * SEG], in0=bbuf[:, 0:2 * SEG],
                                 in1=bbuf[:, 2 * SEG:4 * SEG], op=AluOpType.add)
            c = ve.tensor_tensor(out=yb[:, dg * TC2:(dg + 1) * TC2],
                                 in0=bbuf[:, 1:1 + TC2],
                                 in1=bbuf[:, SEG + 1:SEG + 1 + TC2], op=AluOpType.add)
            ve_done[dg] = ctx.inc(c, "s_ve")
        dg_end[ch] = ve_done[DG_C - 1]

    HEAD(0)
    for ch in range(NCH):
        pc = ch % 2
        t0 = ch * TC2
        DGLOOP(ch)
        if ch + 1 < NCH:
            HEAD(ch + 1)
        sc.wait_ge(ctx.sem("s_ve"), dg_end[ch])
        sc.wait_ge(ctx.sem("d_sy"), load_done[ch])
        for dg in range(DG_C):
            a = sc.activation(out=zb2[:, (pc * 8 + dg) * TC2:(pc * 8 + dg + 1) * TC2],
                              in_=zb2[:, (pc * 8 + dg) * TC2:(pc * 8 + dg + 1) * TC2],
                              func=ACT.Silu)
        ctx.inc(a, "s_sc")
        ctx.after(ve, "s_sc")
        if ch >= 1:
            ve.wait_ge(ctx.sem("s_te"), wout_done[ch - 1])
        for dg in range(DG_C):
            c = ve.tensor_tensor(out=ygb[:, dg * TC2:(dg + 1) * TC2],
                                 in0=yb[:, dg * TC2:(dg + 1) * TC2],
                                 in1=zb2[:, (pc * 8 + dg) * TC2:(pc * 8 + dg + 1) * TC2],
                                 op=AluOpType.mult)
        ygb_done = ctx.inc(c, "s_ve")
        chunk_ve_end[ch] = ygb_done
        te.wait_ge(ctx.sem("s_ve"), ygb_done)
        if ch >= 1:
            te.wait_ge(ctx.sem("s_sc"), oev_ev[ch - 1])
        ntt = TC2 // 128
        for tt in range(ntt):
            for vh in range(2):
                for dg in range(DG_C):
                    m = te.matmul(ps_o[:, (tt % 2 * 2 + vh) * 512:(tt % 2 * 2 + vh + 1) * 512],
                                  ygb[:, dg * TC2 + tt * 128: dg * TC2 + (tt + 1) * 128],
                                  wout[:, dg * 1024 + vh * 512: dg * 1024 + (vh + 1) * 512],
                                  start=(dg == 0), stop=(dg == DG_C - 1))
            ctx.inc(m, "s_te")
            ctx.after(sc, "s_te")
            if ch >= 1:
                sc.wait_ge(ctx.sem("d_sy"), oev_dma[ch - 1])
            a = sc.activation(out=oev[:, (tt % 2) * 1024:(tt % 2 + 1) * 1024],
                              in_=ps_o[:, (tt % 2 * 2) * 512:(tt % 2 * 2 + 2) * 512],
                              func=ACT.Copy)
            ctx.inc(a, "s_sc")
            ctx.after(sync, "s_sc")
            ctx.inc(sync.dma_start(
                out=io["c_part"][t0 + tt * 128: t0 + (tt + 1) * 128, :],
                in_=oev[:, (tt % 2) * 1024:(tt % 2 + 1) * 1024]), "d_sy", 16)
        wout_done[ch] = ctx.val("s_te")
        oev_dma[ch] = ctx.val("d_sy")
        oev_ev[ch] = ctx.val("s_sc")
    st.close()


def host_stage_c_inputs(inp, core):
    b, half = core // 2, core % 2
    ch0 = half * 1024
    Win = inp["L1_attn_W_in"]
    WinT = Win.T
    Win_sl = np.concatenate([WinT[:, ch0:ch0 + 1024],
                             WinT[:, 2048 + ch0:2048 + ch0 + 1024]], axis=1)
    cwv = inp["L1_attn_conv_w"][ch0:ch0 + 1024, 0, :]
    cbv = inp["L1_attn_conv_b"][ch0:ch0 + 1024]
    Wx = inp["L1_attn_W_x"]
    WxT_sl = Wx.T[ch0:ch0 + 1024, :]
    A = inp["L1_attn_A"][ch0:ch0 + 1024]
    Wdt = inp["L1_attn_W_dt"][ch0:ch0 + 1024, 0]
    bdt = inp["L1_attn_b_dt"][ch0:ch0 + 1024]
    Wout = inp["L1_attn_W_out"]
    WoutT_sl = Wout.T[ch0:ch0 + 1024, :]
    dt_cols = np.zeros((128, 16), np.float32)
    dt_cols[:, 0:8] = Wdt.reshape(8, 128).T
    dt_cols[:, 8:16] = bdt.reshape(8, 128).T
    return {
        "c_WinT": np.ascontiguousarray(Win_sl),
        "c_WxT": np.ascontiguousarray(WxT_sl),
        "c_conv_w_cols": np.ascontiguousarray(
            cwv.reshape(8, 128, 4).transpose(1, 0, 2).reshape(128, 32)),
        "c_conv_b_cols": np.ascontiguousarray(cbv.reshape(8, 128).T),
        "c_Acols": np.ascontiguousarray(
            A.reshape(8, 128, 16).transpose(1, 0, 2).reshape(128, 128)),
        "c_dt_cols": dt_cols,
        "c_WoutT": np.ascontiguousarray(WoutT_sl),
    }


def emit_stage_d_prelude(ctx, io):
    from contextlib import ExitStack
    nc = ctx.nc
    sync, gp, ve, sc, te = (ctx.eng[k] for k in ("sync", "gpsimd", "vector", "scalar", "tensor"))
    consts_sb = io["consts_sb"]
    st = ExitStack()
    x2t = st.enter_context(nc.sbuf_tensor("dp_x2t", [128, 2 * 1024], F32))
    p0t = st.enter_context(nc.sbuf_tensor("d_p0t", [128, 2 * 1024], F32))
    p1t = st.enter_context(nc.sbuf_tensor("d_p1t", [128, 2 * 1024], F32))
    x3t = st.enter_context(nc.sbuf_tensor("d_x3t", [128, 1024], F32))
    lncol = st.enter_context(nc.sbuf_tensor("d_lnc", [128, 8], F32))
    sqd = st.enter_context(nc.sbuf_tensor("d_sqd", [128, 1024], F32))
    for tl in range(NT):
        pb = tl % 2
        ctx.after(sync, "s_ve", "s_sc")
        for buf, key in ((x2t, "d_x2ext"), (p0t, "d_p0"), (p1t, "d_p1")):
            ctx.inc(sync.dma_start(out=buf[:, pb * 1024:(pb + 1) * 1024],
                                   in_=io[key][tl * 128:(tl + 1) * 128, :]), "d_sy", 16)
        ctx.after(ve, "d_sy")
        c = ve.tensor_tensor(out=x3t[:], in0=p0t[:, pb * 1024:(pb + 1) * 1024],
                             in1=p1t[:, pb * 1024:(pb + 1) * 1024], op=AluOpType.add)
        c = ve.tensor_tensor(out=x3t[:], in0=x3t[:],
                             in1=x2t[:, pb * 1024:(pb + 1) * 1024], op=AluOpType.add)
        ctx.inc(c, "s_ve")
        ln_epilogue(ctx, ve, sc, x3t[:], lncol, sqd[:], consts_sb, x3t[:])
        ctx.after(sync, "s_ve")
        ctx.inc(sync.dma_start(out=io["d_x3"][tl * 128:(tl + 1) * 128, :], in_=x3t[:]),
                "d_sy", 16)
        ctx.after(ve, "d_sy")
    st.close()


def emit_stage_d_prelude_fused(ctx, io):
    from contextlib import ExitStack
    nc = ctx.nc
    sync, gp, ve, sc, te = (ctx.eng[k] for k in ("sync", "gpsimd", "vector", "scalar", "tensor"))
    consts_sb = io["consts_sb"]
    st = ExitStack()
    x2t = st.enter_context(nc.sbuf_tensor("dp_x2t", [128, 2 * 1024], F32))
    p0t = st.enter_context(nc.sbuf_tensor("dp_p0t", [128, 2 * 1024], F32))
    x3t = st.enter_context(nc.sbuf_tensor("dp_x3t", [128, 1024], F32))
    lncol = st.enter_context(nc.sbuf_tensor("dp_lnc", [128, 8], F32))
    sqd = st.enter_context(nc.sbuf_tensor("dp_sqd", [128, 1024], F32))
    for tl in range(NT):
        pb = tl % 2
        ctx.after(sync, "s_ve", "s_sc")
        ctx.after(gp, "s_ve", "s_sc")
        if tl == 0:
            ctx.inc(gp.dma_start(out=x2t[:, pb * 1024:(pb + 1) * 1024],
                                 in_=io["c_x2bf"][3968:4096, :]), "d_gp", 16)
            ctx.inc(gp.dma_start(out=p0t[:, pb * 1024:(pb + 1) * 1024],
                                 in_=io["c_edge"][0:128, :]), "d_gp", 16)
        elif tl == NT - 1:
            ctx.inc(gp.dma_start(out=x2t[:, pb * 1024:(pb + 1) * 1024],
                                 in_=io["c_x2bf"][4096:4224, :]), "d_gp", 16)
            ctx.inc(gp.dma_start(out=p0t[:, pb * 1024:(pb + 1) * 1024],
                                 in_=io["c_edge"][128:256, :]), "d_gp", 16)
        else:
            ctx.inc(sync.dma_start(out=x2t[:, pb * 1024:(pb + 1) * 1024],
                                   in_=io["b_xout"][tl * 128:(tl + 1) * 128, :]), "d_sy", 16)
            ctx.inc(gp.dma_start(out=p0t[:, pb * 1024:(pb + 1) * 1024],
                                 in_=io["c_rsout"][(tl - 1) * 128: tl * 128, :]), "d_gp", 16)
        ctx.after(ve, "d_sy", "d_gp")
        c = ve.tensor_tensor(out=x3t[:], in0=x2t[:, pb * 1024:(pb + 1) * 1024],
                             in1=p0t[:, pb * 1024:(pb + 1) * 1024], op=AluOpType.add)
        ctx.inc(c, "s_ve")
        ln_epilogue(ctx, ve, sc, x3t[:], lncol, sqd[:], consts_sb, x3t[:])
        ctx.after(sync, "s_ve")
        ctx.inc(sync.dma_start(out=io["d_x3"][tl * 128:(tl + 1) * 128, :], in_=x3t[:]),
                "d_sy", 16)
        ctx.after(ve, "d_sy")
    st.close()


from contextlib import ExitStack


def _common_inputs():
    import ml_dtypes
    return {
        "consts": CONSTS_NP,
        "ident": np.eye(128, dtype=ml_dtypes.bfloat16),
        "identf": np.eye(128, dtype=np.float32),
        "ones_col": np.ones((1, 128), ml_dtypes.bfloat16),
        "ones_kcol": np.ones((128, 1), ml_dtypes.bfloat16),
        "onesf_row": np.ones((1, 128), np.float32),
    }


def _build_fused():
    nc = bass.Bass("TRN2", target_bir_lowering=False, detect_race_conditions=False)
    io = {}
    EI, IN, EO = "ExternalInput", "Internal", "ExternalOutput"

    def dram(name, shape, dt, kind):
        io[name] = nc.dram_tensor(name, shape, dt, kind=kind).ap()

    dram("t_batch", [S, D], F32, EI)
    dram("t_own", [TEXT, D], F32, EI)
    dram("WqkvT", [D, 3072], F32, EI)
    dram("bqkv_cols", [128, 24], F32, EI)
    dram("bv_row", [1, 1024], F32, EI)
    dram("Kscr", [D, S], BF16, IN)
    dram("Vscr", [S, D], BF16, IN)
    dram("Qscr", [D, TEXT], BF16, IN)
    dram("Oscr", [TEXT, D], F32, IN)
    dram("x1", [TEXT, D], F32, IN)
    dram("b_W1T", [1024, 4096], F32, EI)
    dram("b_W2T", [4096, 1024], F32, EI)
    dram("b_conv_w_cols", [128, 96], F32, EI)
    dram("b_conv_b_cols", [128, 32], F32, EI)
    dram("mask_lo", [128, 1], F32, EI)
    dram("mask_hi", [128, 1], F32, EI)
    dram("b_xqT", [1024, TEXT], BF16, IN)
    dram("b_w2qS", [4096, 1024], BF16, IN)
    dram("b_hqT", [4096, TEXT], BF16, IN)
    dram("b_xout", [TEXT, D], F32, IN)
    dram("b_xout_bf", [TOWN, D], BF16, IN)
    io["b_xin"] = io["x1"]
    dram("c_x2bf", [S, D], BF16, IN)
    dram("c_WinT", [1024, 2048], F32, EI)
    dram("c_WxT", [1024, 33], F32, EI)
    dram("c_conv_w_cols", [128, 32], F32, EI)
    dram("c_conv_b_cols", [128, 8], F32, EI)
    dram("c_Acols", [128, 128], F32, EI)
    dram("c_dt_cols", [128, 16], F32, EI)
    dram("c_WoutT", [1024, 1024], F32, EI)
    dram("c_u", [1024, S], F32, IN)
    dram("c_z", [1024, S], BF16, IN)
    dram("c_xdbl", [33, S], F32, IN)
    dram("c_xdbl_sum", [33, S], F32, IN)
    dram("c_part", [S, D], BF16, IN)
    dram("c_rsout", [TOWN, D], BF16, IN)
    dram("c_edgein", [256, D], BF16, IN)
    dram("c_edge", [256, D], BF16, IN)
    dram("d_W1T", [1024, 4096], F32, EI)
    dram("d_W2T", [4096, 1024], F32, EI)
    dram("d_conv_w_cols", [128, 96], F32, EI)
    dram("d_conv_b_cols", [128, 32], F32, EI)
    dram("d_x3", [TEXT, D], F32, IN)
    dram("d_xqT", [1024, TEXT], BF16, IN)
    dram("d_w2qS", [4096, 1024], BF16, IN)
    dram("d_hqT", [4096, TEXT], BF16, IN)
    dram("d_xout", [TEXT, D], F32, EO)
    io["d_xin"] = io["d_x3"]
    declare_common(nc, io)

    ctx = Ctx(nc)
    ctx.eng = {"sync": nc.sync, "gpsimd": nc.gpsimd, "vector": nc.vector,
               "scalar": nc.scalar, "tensor": nc.tensor}
    pairs = [[0, 1], [2, 3], [4, 5], [6, 7]]
    with ExitStack() as stack:
        ctx._stack = stack
        with nc.Block():
            load_common(ctx, io)
            ctx.set_stage("a")
            with nc.named_scope("stageA"):
                emit_stage_a(ctx, io)
            ctx.stage_barrier()
            ctx.set_stage("b")
            with nc.named_scope("stageB"):
                emit_stage_ffn(ctx, io, "b")
            ctx.stage_barrier()
            ctx.set_stage("c")
            gp = ctx.eng["gpsimd"]
            g = gp.collective_compute("AllGather", mybir.AluOpType.bypass,
                                      replica_groups=pairs,
                                      ins=[io["b_xout_bf"]], outs=[io["c_x2bf"]])
            ctx.inc(g, "s_gp")
            for e in ctx.eng.values():
                ctx.after(e, "s_gp")
            with nc.named_scope("stageC1"):
                emit_stage_c1(ctx, io)
            ctx.after(gp, "d_sy")
            g = gp.collective_compute("AllReduce", mybir.AluOpType.add,
                                      replica_groups=pairs,
                                      ins=[io["c_xdbl"]], outs=[io["c_xdbl_sum"]])
            ctx.inc(g, "s_gp")
            for e in ctx.eng.values():
                ctx.after(e, "s_gp")
            with nc.named_scope("stageC2"):
                emit_stage_c2(ctx, io)
            ctx.stage_barrier()
            ctx.set_stage("d")
            sync = ctx.eng["sync"]
            ctx.inc(sync.dma_start(out=io["c_edgein"], in_=io["c_part"][3968:4224, :]),
                    "d_sy", 16)
            ctx.after(gp, "d_sy")
            g = gp.collective_compute("ReduceScatter", mybir.AluOpType.add,
                                      replica_groups=pairs,
                                      ins=[io["c_part"]], outs=[io["c_rsout"]])
            ctx.inc(g, "s_gp")
            g = gp.collective_compute("AllReduce", mybir.AluOpType.add,
                                      replica_groups=pairs,
                                      ins=[io["c_edgein"]], outs=[io["c_edge"]])
            ctx.inc(g, "s_gp")
            for e in ctx.eng.values():
                ctx.after(e, "s_gp")
            with nc.named_scope("stageDp"):
                emit_stage_d_prelude_fused(ctx, io)
            with nc.named_scope("stageD"):
                emit_stage_ffn(ctx, io, "d")
            finish(ctx)
    return nc


def kernel(**inputs):
    from concourse.bass_utils import run_bass_kernel_spmd
    common = _common_inputs()
    cores = list(range(8))
    nc = _build_fused()
    maps = []
    for c in cores:
        m = {}
        m.update(host_stage_a_inputs(inputs, c))
        m.update(host_stage_ffn_inputs(inputs, c, "L0_ffn", "b"))
        m.update(host_stage_c_inputs(inputs, c))
        md = host_stage_ffn_inputs(inputs, c, "L1_ffn", "d")
        m["d_W1T"] = md["d_W1T"]
        m["d_W2T"] = md["d_W2T"]
        m["d_conv_w_cols"] = md["d_conv_w_cols"]
        m["d_conv_b_cols"] = md["d_conv_b_cols"]
        m.update(common)
        maps.append(m)
    res = run_bass_kernel_spmd(nc, maps, core_ids=cores)
    out = np.empty((4, S, D), np.float32)
    for c in cores:
        b, half = c // 2, c % 2
        out[b, half * TOWN:(half + 1) * TOWN] = \
            np.asarray(res.results[c]["d_xout"])[128:128 + TOWN]
    return out


# version 12
# speedup vs baseline: 1.6133x; 1.0265x over previous
HARD_SYNC = False
"""Megakernel builder for mk_seq_postln_d2 (v1: correctness-first, strong waits).

Layouts:
  TM (token-major): [128 tokens on partitions, features on free]
  FM (feature-major): [128 features on partitions, tokens on free]
Per core c: batch b=c//2, half=c%2, own tokens [half*4096,(half+1)*4096),
extended range adds 128 tokens each side (zero-padded at batch edges,
invalid rows killed via mask columns in stages B/D).

Weights host-pre-transposed. Matmuls bf16 x bf16 -> fp32 PSUM.
"""
import numpy as np
import ml_dtypes
import concourse.bass as bass
import concourse.mybir as mybir
from concourse.alu_op_type import AluOpType

F32 = mybir.dt.float32
BF16 = mybir.dt.bfloat16
I32 = mybir.dt.int32
ACT = mybir.ActivationFunctionType
AX = mybir.AxisListType

D = 1024
DC = 8
S = 8192
TOWN = 4096
TEXT = 4352
NT = TEXT // 128

C_NEGHALF, C_NEGONE, C_INV32, C_EPS, C_INV1024, C_HALF, C_QCLIP, C_127, \
    C_INV127, C_ONE, C_ZERO, C_INVWELEM, C_SP1, C_SP2, C_SP3, C_SP4 = range(16)

CONSTS_NP = np.zeros((128, 16), np.float32)
CONSTS_NP[:, C_NEGHALF] = -0.5
CONSTS_NP[:, C_NEGONE] = -1.0
CONSTS_NP[:, C_INV32] = 1.0 / 32.0
CONSTS_NP[:, C_EPS] = 1e-5
CONSTS_NP[:, C_INV1024] = 1.0 / 1024.0
CONSTS_NP[:, C_HALF] = 0.5
CONSTS_NP[:, C_QCLIP] = 1e-5
CONSTS_NP[:, C_127] = 127.0
CONSTS_NP[:, C_INV127] = 1.0 / 127.0
CONSTS_NP[:, C_ONE] = 1.0
CONSTS_NP[:, C_INVWELEM] = 1.0 / (4096.0 * 1024.0)


class Ctx:

    def __init__(self, nc):
        self.nc = nc
        self.sems = {}
        self.counts = {}
        self.eng = {}
        self._stack = None
        self.suffix = ""

    def set_stage(self, suffix):
        self.suffix = suffix

    def sem(self, name):
        name = name + self.suffix
        if name not in self.sems:
            h = self._stack.enter_context(self.nc.semaphore(name))
            self.sems[name] = h
            self.counts[name] = 0
        return self.sems[name]

    def inc(self, inst, name, n=1):
        inst.then_inc(self.sem(name), n)
        self.counts[name + self.suffix] += n
        return self.counts[name + self.suffix]

    def hard_sync(self):
        for e in self.eng.values():
            for s in list(self.counts):
                if self.counts[s] > 0:
                    e.wait_ge(self.sems[s], self.counts[s])

    def stage_barrier(self):
        self.hard_sync()

    def val(self, name):
        self.sem(name)
        return self.counts[name + self.suffix]

    def after(self, consumer, *sem_names):
        for s0 in sem_names:
            s = s0 + self.suffix
            if s in self.counts and self.counts[s] > 0:
                consumer.wait_ge(self.sems[s], self.counts[s])

ALL_SEMS = ("d_sy", "d_gp", "s_te", "s_ve", "s_sc", "s_gp")


def ln_epilogue(ctx, ve, sc, x_ap, lncol, sqdump, consts_sb, out_ap):
    c = ve.tensor_reduce(out=lncol[:, 0:1], in_=x_ap, axis=AX.X, op=AluOpType.add)
    ctx.inc(c, "s_ve")
    ve.wait_ge(ctx.sem("s_ve"), ctx.counts["s_ve" + ctx.suffix])
    c = ve.tensor_scalar(out=lncol[:, 1:2], in0=lncol[:, 0:1],
                         scalar1=1.0 / 1024.0, scalar2=None, op0=AluOpType.mult)
    ctx.inc(c, "s_ve")
    ve.wait_ge(ctx.sem("s_ve"), ctx.counts["s_ve" + ctx.suffix])
    c = ve.tensor_scalar(out=x_ap, in0=x_ap, scalar1=lncol[:, 1:2], scalar2=None,
                         op0=AluOpType.subtract)
    ctx.inc(c, "s_ve")
    ctx.after(sc, "s_ve")
    a = sc.activation(out=sqdump, in_=x_ap, func=ACT.Square, accum_out=lncol[:, 2:3])
    ctx.inc(a, "s_sc")
    sc.wait_ge(ctx.sem("s_sc"), ctx.counts["s_sc" + ctx.suffix])
    a = sc.activation(out=lncol[:, 3:4], in_=lncol[:, 2:3], func=ACT.Sqrt,
                      scale=consts_sb[:, C_INV1024:C_INV1024 + 1],
                      bias=consts_sb[:, C_EPS:C_EPS + 1])
    ctx.inc(a, "s_sc")
    ctx.after(ve, "s_sc")
    c = ve.reciprocal(lncol[:, 4:5], lncol[:, 3:4])
    ctx.inc(c, "s_ve")
    ve.wait_ge(ctx.sem("s_ve"), ctx.counts["s_ve" + ctx.suffix])
    c = ve.tensor_scalar(out=out_ap, in0=x_ap, scalar1=lncol[:, 4:5], scalar2=None,
                         op0=AluOpType.mult)
    ctx.inc(c, "s_ve")


def emit_stage_a(ctx, io):
    nc = ctx.nc
    sync, gp, ve, sc, te = (ctx.eng[k] for k in ("sync", "gpsimd", "vector", "scalar", "tensor"))
    consts_sb = io["consts_sb"]

    with (
        nc.sbuf_tensor("a_wq", [128, 8 * 3072], BF16) as wq,
        nc.sbuf_tensor("a_bqc", [128, 24], F32) as bqc,
        nc.sbuf_tensor("a_bvr", [1, 1024], BF16) as bvr,
        nc.sbuf_tensor("a_tt", [128, 4 * 1024], F32) as t_tm,
        nc.sbuf_tensor("a_tb", [128, 4 * 1024], BF16) as t_bf,
        nc.sbuf_tensor("a_tT", [128, 8 * 512], BF16) as t_fm,
        nc.sbuf_tensor("a_ev", [128, 2 * 512], BF16) as ev,
        nc.sbuf_tensor("a_vev", [128, 2 * 1024], BF16) as vev,
        nc.psum_tensor("a_ps_t", [128, 512], BF16) as ps_t,
        nc.psum_tensor("a_ps_p", [128, 2 * 512], F32) as ps_p,
    ):
        for dc in range(DC):
            ctx.inc(gp.dma_start(out=wq[:, dc * 3072:(dc + 1) * 3072],
                                 in_=io["WqkvT"][dc * 128:(dc + 1) * 128, :]), "d_gp", 16)
        ctx.inc(sync.dma_start(out=bqc[:], in_=io["bqkv_cols"]), "d_sy", 16)
        ctx.inc(gp.dma_start(out=bvr[:], in_=io["bv_row"]), "d_gp", 16)
        ctx.after(te, "d_gp", "d_sy")
        ctx.after(sc, "d_gp", "d_sy")

        def proj_block(src_ap, row0, n_tt, do_q, do_kv, col0):
            nt128 = n_tt * 128
            ctx.after(sync, "s_sc")
            for i in range(n_tt):
                ctx.inc(sync.dma_start(out=t_tm[:, i * 1024:(i + 1) * 1024],
                                       in_=src_ap[row0 + i * 128: row0 + (i + 1) * 128, :]),
                        "d_sy", 16)
            ctx.after(sc, "d_sy", "s_te")
            for i in range(n_tt):
                a = sc.activation(out=t_bf[:, i * 1024:(i + 1) * 1024],
                                  in_=t_tm[:, i * 1024:(i + 1) * 1024], func=ACT.Copy)
            ctx.inc(a, "s_sc")
            ctx.after(te, "s_sc")
            for dc in range(DC):
                ctx.after(te, "s_ve")
                for i in range(n_tt):
                    m = te.transpose(ps_t[:, i * 128:(i + 1) * 128],
                                     t_bf[:, i * 1024 + dc * 128: i * 1024 + (dc + 1) * 128],
                                     io["ident_sb"])
                ctx.inc(m, "s_te")
                ctx.after(ve, "s_te")
                c = ve.tensor_copy(t_fm[:, dc * 512: dc * 512 + nt128], ps_t[:, 0:nt128])
                ctx.inc(c, "s_ve")
            ctx.after(te, "s_ve")

            def proj_fm(ci_feat0, bias_col, scr, scr_col0):
                mm_done = [None] * DC
                ev_done = [None] * DC
                dma_done = [None] * DC

                def emit_mm(ci):
                    pb = ci % 2
                    for dc in range(DC):
                        m = te.matmul(ps_p[:, pb * 512: pb * 512 + nt128],
                                      wq[:, dc * 3072 + ci_feat0 + ci * 128: dc * 3072 + ci_feat0 + (ci + 1) * 128],
                                      t_fm[:, dc * 512: dc * 512 + nt128],
                                      start=(dc == 0), stop=(dc == DC - 1))
                    mm_done[ci] = ctx.inc(m, "s_te")

                ctx.after(te, "d_sy", "s_sc")
                emit_mm(0)
                for ci in range(DC):
                    pb = ci % 2
                    if ci + 1 < DC:
                        if ci >= 1:
                            te.wait_ge(ctx.sem("s_sc"), ev_done[ci - 1])
                        emit_mm(ci + 1)
                    sc.wait_ge(ctx.sem("s_te"), mm_done[ci])
                    if ci >= 2:
                        sc.wait_ge(ctx.sem("d_sy"), dma_done[ci - 2])
                    a = sc.activation(out=ev[:, pb * 512: pb * 512 + nt128],
                                      in_=ps_p[:, pb * 512: pb * 512 + nt128],
                                      func=ACT.Identity, bias=bqc[:, bias_col + ci: bias_col + ci + 1])
                    ev_done[ci] = ctx.inc(a, "s_sc")
                    sync.wait_ge(ctx.sem("s_sc"), ev_done[ci])
                    dma_done[ci] = ctx.inc(sync.dma_start(
                        out=scr[ci * 128:(ci + 1) * 128, scr_col0: scr_col0 + nt128],
                        in_=ev[:, pb * 512: pb * 512 + nt128]), "d_sy", 16)

            if do_kv:
                proj_fm(1024, 8, io["Kscr"], col0)
                for i in range(n_tt):
                    pb = i % 2
                    ctx.after(te, "d_sy")
                    for vh in range(2):
                        for dc in range(DC):
                            m = te.matmul(ps_p[:, vh * 512:(vh + 1) * 512],
                                          t_fm[:, dc * 512 + i * 128: dc * 512 + (i + 1) * 128],
                                          wq[:, dc * 3072 + 2048 + vh * 512: dc * 3072 + 2048 + (vh + 1) * 512],
                                          start=(dc == 0), stop=False)
                        m = te.matmul(ps_p[:, vh * 512:(vh + 1) * 512],
                                      io["ones_col_sb"], bvr[:, vh * 512:(vh + 1) * 512],
                                      start=False, stop=True)
                    ctx.inc(m, "s_te")
                    ctx.after(ve, "s_te", "d_sy")
                    c = ve.tensor_copy(vev[:, pb * 1024:(pb + 1) * 1024], ps_p[:, 0:1024])
                    ctx.inc(c, "s_ve")
                    ctx.after(sync, "s_ve")
                    ctx.inc(sync.dma_start(
                        out=io["Vscr"][col0 + i * 128: col0 + (i + 1) * 128, :],
                        in_=vev[:, pb * 1024:(pb + 1) * 1024]), "d_sy", 16)
                    ctx.after(te, "s_ve")
            if do_q:
                proj_fm(0, 0, io["Qscr"], col0)

        for blk in range(16):
            proj_block(io["t_batch"], blk * 512, 4, do_q=False, do_kv=True, col0=blk * 512)
        for blk in range(17):
            proj_block(io["t_own"], blk * 256, 2, do_q=True, do_kv=False, col0=blk * 256)
        ctx.after(sync, "d_sy")
        ctx.after(gp, "d_sy")

    QG = 256
    NQG = TEXT // QG
    with (
        nc.sbuf_tensor("a_ksb", [128, 8 * 4096], BF16) as ksb,
        nc.sbuf_tensor("a_vsb", [128, 32 * 1024], BF16) as vsb,
        nc.sbuf_tensor("a_qsb", [128, 2 * 8 * QG], BF16) as qsb,
        nc.sbuf_tensor("a_pt", [128, 2 * QG], BF16) as ptb,
        nc.sbuf_tensor("a_oev", [128, 2 * 1024], F32) as oev,
        nc.sbuf_tensor("a_opr", [128, 1024], F32) as opr,
        nc.sbuf_tensor("a_res", [128, 1024], F32) as resid,
        nc.sbuf_tensor("a_rs", [128, 2128], F32) as rssb_pad,
        nc.sbuf_tensor("a_x1", [128, 1024], F32) as x1t,
        nc.sbuf_tensor("a_lncol", [128, 8], F32) as lncol,
        nc.sbuf_tensor("a_sq", [128, 1024], F32) as sqdump,
        nc.psum_tensor("a_ps_s", [128, 2 * 512], F32) as ps_s,
        nc.psum_tensor("a_ps_o", [128, 4 * 512], F32) as ps_o,
        nc.psum_tensor("a_ps_r", [128, 2 * 512], F32) as ps_r_,
    ):
        rssb = rssb_pad[:, 1024:1104]
        ps_r = ps_r_
        for kh in range(2):
            ctx.after(sync, "s_te")
            ctx.after(gp, "s_te")
            for dc in range(DC):
                ctx.inc(sync.dma_start(out=ksb[:, dc * 4096:(dc + 1) * 4096],
                                       in_=io["Kscr"][dc * 128:(dc + 1) * 128, kh * 4096:(kh + 1) * 4096]),
                        "d_sy", 16)
            for vt in range(32):
                ctx.inc(sync.dma_start(
                    out=vsb[:, vt * 1024:(vt + 1) * 1024],
                    in_=io["Vscr"][kh * 4096 + vt * 128: kh * 4096 + (vt + 1) * 128, :]),
                    "d_sy", 16)
            ctx.after(te, "d_sy", "d_gp")

            for qg in range(NQG):
                qb = qg % 2
                ctx.after(sync, "s_te")
                for dc in range(DC):
                    ctx.inc(sync.dma_start(out=qsb[:, (qb * 8 + dc) * QG:(qb * 8 + dc + 1) * QG],
                                           in_=io["Qscr"][dc * 128:(dc + 1) * 128, qg * QG:(qg + 1) * QG]),
                            "d_sy", 16)
                ctx.after(te, "d_sy")
                exp_done = [None] * 32
                qk_done = [None] * 32

                def emit_qk(kt):
                    sb = kt % 2
                    for dc in range(DC):
                        m = te.matmul(ps_s[:, sb * 512: sb * 512 + 256],
                                      ksb[:, dc * 4096 + kt * 128: dc * 4096 + (kt + 1) * 128],
                                      qsb[:, (qb * 8 + dc) * QG:(qb * 8 + dc + 1) * QG],
                                      start=(dc == 0), stop=(dc == DC - 1))
                    qk_done[kt] = ctx.inc(m, "s_te")

                emit_qk(0)
                for kt in range(32):
                    sb = kt % 2
                    sc.wait_ge(ctx.sem("s_te"), qk_done[kt])
                    a = sc.activation(out=ptb[:, sb * 256:(sb + 1) * 256],
                                      in_=ps_s[:, sb * 512: sb * 512 + 256],
                                      func=ACT.Exp, scale=consts_sb[:, C_INV32:C_INV32 + 1])
                    exp_done[kt] = ctx.inc(a, "s_sc")
                    if kt + 1 < 32:
                        if kt >= 1:
                            te.wait_ge(ctx.sem("s_sc"), exp_done[kt - 1])
                        emit_qk(kt + 1)
                    te.wait_ge(ctx.sem("s_sc"), exp_done[kt])
                    for qt in range(2):
                        for vh in range(2):
                            m = te.matmul(ps_o[:, (qt * 2 + vh) * 512:(qt * 2 + vh + 1) * 512],
                                          ptb[:, sb * 256 + qt * 128: sb * 256 + (qt + 1) * 128],
                                          vsb[:, kt * 1024 + vh * 512: kt * 1024 + (vh + 1) * 512],
                                          start=(kt == 0), stop=(kt == 31))
                        m2 = te.matmul(ps_r[:, qt * 512: qt * 512 + 1],
                                       ptb[:, sb * 256 + qt * 128: sb * 256 + (qt + 1) * 128],
                                       io["ones_kcol_sb"],
                                       start=(kt == 0), stop=(kt == 31))
                    ctx.inc(m2, "s_te")
                if HARD_SYNC:
                    ctx.hard_sync()
                ctx.after(ve, "s_te")
                if kh == 0:
                    c = ve.tensor_copy(oev[:, 0:2048], ps_o[:, 0:2048])
                    c = ve.tensor_copy(rssb[:, qg * 2: qg * 2 + 1], ps_r[:, 0:1])
                    c = ve.tensor_copy(rssb[:, 40 + qg: 41 + qg], ps_r[:, 512:513])
                    ctx.inc(c, "s_ve")
                    ctx.after(sync, "s_ve")
                    for qt in range(2):
                        ctx.inc(sync.dma_start(
                            out=io["Oscr"][(qg * 2 + qt) * 128:(qg * 2 + qt + 1) * 128, :],
                            in_=oev[:, qt * 1024:(qt + 1) * 1024]), "d_sy", 16)
                    ctx.after(te, "s_ve")
                else:
                    for qt in range(2):
                        row = (qg * 2 + qt) * 128
                        ctx.after(sync, "s_ve")
                        ctx.inc(sync.dma_start(out=opr[:], in_=io["Oscr"][row:row + 128, :]),
                                "d_sy", 16)
                        ctx.inc(sync.dma_start(out=resid[:], in_=io["t_own"][row:row + 128, :]),
                                "d_sy", 16)
                        ctx.after(ve, "d_sy")
                        c = ve.tensor_tensor(out=rssb[:, 76 + qt: 77 + qt],
                                             in0=(rssb[:, qg * 2: qg * 2 + 1] if qt == 0
                                                  else rssb[:, 40 + qg: 41 + qg]),
                                             in1=ps_r[:, qt * 512: qt * 512 + 1], op=AluOpType.add)
                        ctx.inc(c, "s_ve")
                        ve.wait_ge(ctx.sem("s_ve"), ctx.counts["s_ve" + ctx.suffix])
                        c = ve.reciprocal(rssb[:, 78 + qt: 79 + qt], rssb[:, 76 + qt: 77 + qt])
                        ctx.inc(c, "s_ve")
                        ve.wait_ge(ctx.sem("s_ve"), ctx.counts["s_ve" + ctx.suffix])
                        c = ve.tensor_tensor(out=oev[:, 0:1024],
                                             in0=ps_o[:, qt * 1024:(qt + 1) * 1024],
                                             in1=opr[:], op=AluOpType.add)
                        ctx.inc(c, "s_ve")
                        ve.wait_ge(ctx.sem("s_ve"), ctx.counts["s_ve" + ctx.suffix])
                        c = ve.scalar_tensor_tensor(out=x1t[:], in0=oev[:, 0:1024],
                                                    scalar=rssb[:, 78 + qt: 79 + qt],
                                                    in1=resid[:],
                                                    op0=AluOpType.mult, op1=AluOpType.add)
                        ctx.inc(c, "s_ve")
                        ln_epilogue(ctx, ve, sc, x1t[:], lncol, sqdump[:], consts_sb, x1t[:])
                        ctx.after(sync, "s_ve")
                        ctx.inc(sync.dma_start(out=io["x1"][row:row + 128, :], in_=x1t[:]),
                                "d_sy", 16)
                    ctx.after(te, "s_ve")


def finish(ctx):
    sync = ctx.eng["sync"]
    for s in list(ctx.counts):
        if (s.startswith("d_sy") or s.startswith("d_gp")) and ctx.counts[s]:
            sync.wait_ge(ctx.sems[s], ctx.counts[s])



def host_stage_a_inputs(inp, core):
    b, half = core // 2, core % 2
    t = inp["L0_attn_t"][b]
    start = half * TOWN
    t_own = np.zeros((TEXT, D), np.float32)
    lo = start - 128
    s0 = max(lo, 0)
    e0 = min(start + TOWN + 128, S)
    t_own[s0 - lo: e0 - lo] = t[s0:e0]
    Wq = inp["L0_attn_W_qkv"]
    b_qkv = inp["L0_attn_b_qkv"]
    return {
        "t_batch": np.ascontiguousarray(t),
        "t_own": t_own,
        "WqkvT": np.ascontiguousarray(Wq.T),
        "bqkv_cols": np.ascontiguousarray(b_qkv.reshape(24, 128).T),
        "bv_row": np.ascontiguousarray(b_qkv[2048:].reshape(1, 1024)),
        "consts": CONSTS_NP,
        "ident": np.eye(128, dtype=ml_dtypes.bfloat16),
        "identf": np.eye(128, dtype=np.float32),
        "onesf_row": np.ones((1, 128), np.float32),
        "ones_col": np.ones((1, 128), ml_dtypes.bfloat16),
        "ones_kcol": np.ones((128, 1), ml_dtypes.bfloat16),
    }


def declare_common(nc, io):
    consts = nc.dram_tensor("consts", [128, 16], F32, kind="ExternalInput").ap()
    ident = nc.dram_tensor("ident", [128, 128], BF16, kind="ExternalInput").ap()
    identf = nc.dram_tensor("identf", [128, 128], F32, kind="ExternalInput").ap()
    onesf_row = nc.dram_tensor("onesf_row", [1, 128], F32, kind="ExternalInput").ap()
    ones_col = nc.dram_tensor("ones_col", [1, 128], BF16, kind="ExternalInput").ap()
    ones_kcol = nc.dram_tensor("ones_kcol", [128, 1], BF16, kind="ExternalInput").ap()
    io.update(consts=consts, ident=ident, identf=identf, onesf_row=onesf_row,
              ones_col=ones_col, ones_kcol=ones_kcol)
    return io


def load_common(ctx, io):
    nc = ctx.nc
    sync = ctx.eng["sync"]
    cs = ctx._stack.enter_context(nc.sbuf_tensor("c_consts", [128, 16], F32))
    idn = ctx._stack.enter_context(nc.sbuf_tensor("c_ident", [128, 128], BF16))
    idnf = ctx._stack.enter_context(nc.sbuf_tensor("c_identf", [128, 128], F32))
    onfr = ctx._stack.enter_context(nc.sbuf_tensor("c_onesfr", [1, 128], F32))
    oc = ctx._stack.enter_context(nc.sbuf_tensor("c_ones", [1, 128], BF16))
    okc = ctx._stack.enter_context(nc.sbuf_tensor("c_oneskc", [128, 1], BF16))
    ctx.inc(sync.dma_start(out=cs[:], in_=io["consts"]), "d_sy", 16)
    ctx.inc(sync.dma_start(out=idn[:], in_=io["ident"]), "d_sy", 16)
    ctx.inc(sync.dma_start(out=idnf[:], in_=io["identf"]), "d_sy", 16)
    ctx.inc(sync.dma_start(out=onfr[:], in_=io["onesf_row"]), "d_sy", 16)
    ctx.inc(sync.dma_start(out=oc[:], in_=io["ones_col"]), "d_sy", 16)
    ctx.inc(sync.dma_start(out=okc[:], in_=io["ones_kcol"]), "d_sy", 16)
    io["consts_sb"] = cs
    io["ident_sb"] = idn[:]
    io["identf_sb"] = idnf[:]
    io["onesf_row_sb"] = onfr[:]
    io["ones_col_sb"] = oc[:]
    io["ones_kcol_sb"] = okc[:]
    for e in ("tensor", "vector", "scalar", "gpsimd"):
        ctx.after(ctx.eng[e], "d_sy")


def quant_rows(ctx, io, x_ap, consts_sb, qb, out_bf_ap, r_col_out):
    ve, sc = ctx.eng["vector"], ctx.eng["scalar"]
    c = ve.tensor_reduce(out=qb["m"][:, 0:1], in_=x_ap, axis=AX.X,
                         op=AluOpType.max, apply_absolute_value=True)
    ctx.inc(c, "s_ve")
    ve.wait_ge(ctx.sem("s_ve"), ctx.counts["s_ve" + ctx.suffix])
    c = ve.tensor_scalar(out=qb["m"][:, 1:2], in0=qb["m"][:, 0:1], scalar1=1e-5,
                         scalar2=None, op0=AluOpType.max)
    ctx.inc(c, "s_ve")
    ve.wait_ge(ctx.sem("s_ve"), ctx.counts["s_ve" + ctx.suffix])
    c = ve.reciprocal(qb["m"][:, 2:3], qb["m"][:, 1:2])
    ctx.inc(c, "s_ve")
    ve.wait_ge(ctx.sem("s_ve"), ctx.counts["s_ve" + ctx.suffix])
    c = ve.tensor_scalar(out=qb["m"][:, 3:4], in0=qb["m"][:, 2:3], scalar1=127.0,
                         scalar2=None, op0=AluOpType.mult)
    c = ve.tensor_scalar(out=r_col_out, in0=qb["m"][:, 1:2], scalar1=1.0 / 127.0,
                         scalar2=None, op0=AluOpType.mult)
    ctx.inc(c, "s_ve")
    ctx.after(sc, "s_ve")
    a = sc.activation(out=qb["ci"], in_=x_ap, func=ACT.Identity, scale=qb["m"][:, 3:4])
    ctx.inc(a, "s_sc")
    ctx.after(ve, "s_sc")
    c = ve.scalar_tensor_tensor(out=qb["f"], in0=x_ap, scalar=qb["m"][:, 3:4],
                                in1=qb["ci"], op0=AluOpType.mult, op1=AluOpType.subtract)
    ctx.inc(c, "s_ve")
    ctx.after(sc, "s_ve")
    a = sc.activation(out=qb["a1"], in_=qb["f"], func=ACT.Sign,
                      bias=consts_sb[:, C_NEGHALF:C_NEGHALF + 1], scale=1.0)
    ctx.inc(a, "s_sc")
    sc.wait_ge(ctx.sem("s_sc"), ctx.counts["s_sc" + ctx.suffix])
    a = sc.activation(out=qb["f"], in_=qb["f"], func=ACT.Sign,
                      bias=consts_sb[:, C_HALF:C_HALF + 1], scale=1.0)
    ctx.inc(a, "s_sc")
    ctx.after(ve, "s_sc")
    c = ve.tensor_tensor(out=qb["a1"], in0=qb["a1"], in1=qb["f"], op=AluOpType.add)
    c = ve.scalar_tensor_tensor(out=qb["f"], in0=qb["a1"], scalar=0.5, in1=qb["ci"],
                                op0=AluOpType.mult, op1=AluOpType.add)
    ctx.inc(c, "s_ve")
    ve.wait_ge(ctx.sem("s_ve"), ctx.counts["s_ve" + ctx.suffix])
    c = ve.tensor_scalar(out=out_bf_ap, in0=qb["f"], scalar1=qb["m"][:, 1:2],
                         scalar2=1.0 / 127.0, op0=AluOpType.mult, op1=AluOpType.mult)
    ctx.inc(c, "s_ve")

def emit_stage_ffn(ctx, io, pfx):
    from contextlib import ExitStack
    nc = ctx.nc
    sync, gp, ve, sc, te = (ctx.eng[k] for k in ("sync", "gpsimd", "vector", "scalar", "tensor"))
    consts_sb = io["consts_sb"]
    P = pfx
    NBLK = NT // 2

    stW = ExitStack()
    w1q = stW.enter_context(nc.sbuf_tensor(P + "_w1q", [128, 8 * 4096], BF16))
    scl = stW.enter_context(nc.sbuf_tensor(P + "_sc", [128, 8], F32))
    cw = stW.enter_context(nc.sbuf_tensor(P + "_cw", [128, 96], F32))
    cb = stW.enter_context(nc.sbuf_tensor(P + "_cb", [128, 32], F32))
    msk = stW.enter_context(nc.sbuf_tensor(P + "_msk", [128, 2], F32))
    acc = stW.enter_context(nc.sbuf_tensor(P + "_acc", [128, 40], F32))

    ctx.inc(sync.dma_start(out=cw[:], in_=io[P + "_conv_w_cols"]), "d_sy", 16)
    ctx.inc(sync.dma_start(out=cb[:], in_=io[P + "_conv_b_cols"]), "d_sy", 16)
    ctx.inc(sync.dma_start(out=msk[:, 0:1], in_=io["mask_lo"]), "d_sy", 16)
    ctx.inc(sync.dma_start(out=msk[:, 1:2], in_=io["mask_hi"]), "d_sy", 16)

    stQ = ExitStack()
    wtb = stQ.enter_context(nc.sbuf_tensor(P + "_wqb", [128, 3 * 1024], F32))
    wdump = stQ.enter_context(nc.sbuf_tensor(P + "_wqd", [128, 1024], BF16))
    wci = stQ.enter_context(nc.sbuf_tensor(P + "_wqi", [128, 2 * 1024], I32))
    wf = stQ.enter_context(nc.sbuf_tensor(P + "_wqf", [128, 2 * 1024], F32))
    wa1 = stQ.enter_context(nc.sbuf_tensor(P + "_wa1", [128, 2 * 1024], F32))
    wq_ev = stQ.enter_context(nc.sbuf_tensor(P + "_wev", [128, 2 * 1024], BF16))
    ps_w = stQ.enter_context(nc.psum_tensor(P + "_ps_w", [128, 512], F32))

    def wquant(src_slice, n_slices, mbar_col, ws_col, sink, snk_dma):
        ctx.after(sync, "s_ve", "s_sc")
        abs_done = [None] * n_slices
        for i in range(n_slices):
            pb = i % 3
            if i >= 3:
                sync.wait_ge(ctx.sem("s_sc"), abs_done[i - 3])
            ctx.inc(sync.dma_start(out=wtb[:, pb * 1024:(pb + 1) * 1024],
                                   in_=src_slice(i)), "d_sy", 16)
            ld = ctx.val("d_sy")
            sc.wait_ge(ctx.sem("d_sy"), ld)
            a = sc.activation(out=wdump[:], in_=wtb[:, pb * 1024:(pb + 1) * 1024],
                              func=ACT.Abs, accum_out=acc[:, i: i + 1])
            abs_done[i] = ctx.inc(a, "s_sc")
        ctx.after(ve, "s_sc")
        c = ve.tensor_reduce(out=acc[:, 36:37], in_=acc[:, 0:n_slices], axis=AX.X,
                             op=AluOpType.add)
        ctx.inc(c, "s_ve")
        ctx.after(te, "s_ve", "s_sc")
        m = te.matmul(ps_w[0:1, 0:1], acc[:, 36:37],
                      consts_sb[:, C_ONE:C_ONE + 1], start=True, stop=True)
        ctx.inc(m, "s_te")
        ctx.after(sc, "s_te")
        a = sc.activation(out=acc[0:1, 38:39], in_=ps_w[0:1, 0:1], func=ACT.Copy)
        ctx.inc(a, "s_sc")
        ctx.after(te, "s_sc")
        m = te.matmul(ps_w[:, 1:2], io["onesf_row_sb"], acc[0:1, 38:39],
                      start=True, stop=True)
        ctx.inc(m, "s_te")
        ctx.after(ve, "s_te")
        c = ve.tensor_copy(acc[:, 37:38], ps_w[:, 1:2])
        ctx.inc(c, "s_ve")
        ve.wait_ge(ctx.sem("s_ve"), ctx.counts["s_ve" + ctx.suffix])
        nelem = float(n_slices * 128 * 1024)
        c = ve.tensor_scalar(out=mbar_col, in0=acc[:, 37:38], scalar1=1.0 / nelem,
                             scalar2=1e-5, op0=AluOpType.mult, op1=AluOpType.max)
        ctx.inc(c, "s_ve")
        ve.wait_ge(ctx.sem("s_ve"), ctx.counts["s_ve" + ctx.suffix])
        c = ve.reciprocal(ws_col, mbar_col)
        ctx.inc(c, "s_ve")
        stt_done = [None] * n_slices
        ci_done = [None] * n_slices
        sg1_done = [None] * n_slices
        sg2_done = [None] * n_slices
        fin_done = [None] * n_slices
        for i in range(n_slices):
            pb = i % 3
            qs = (i % 2) * 1024
            if i >= 3:
                sync.wait_ge(ctx.sem("s_ve"), stt_done[i - 3])
            else:
                sync.wait_ge(ctx.sem("s_sc"), abs_done[n_slices - 1])
            ctx.inc(sync.dma_start(out=wtb[:, pb * 1024:(pb + 1) * 1024],
                                   in_=src_slice(i)), "d_sy", 16)
            ld = ctx.val("d_sy")
            wt = wtb[:, pb * 1024:(pb + 1) * 1024]
            sc.wait_ge(ctx.sem("d_sy"), ld)
            if i == 0:
                ctx.after(sc, "s_ve")
            if i >= 2:
                sc.wait_ge(ctx.sem("s_ve"), fin_done[i - 2])
            a = sc.activation(out=wci[:, qs:qs + 1024], in_=wt, func=ACT.Identity,
                              scale=ws_col)
            ci_done[i] = ctx.inc(a, "s_sc")
            ve.wait_ge(ctx.sem("s_sc"), ci_done[i])
            c = ve.scalar_tensor_tensor(out=wf[:, qs:qs + 1024], in0=wt, scalar=ws_col,
                                        in1=wci[:, qs:qs + 1024],
                                        op0=AluOpType.mult, op1=AluOpType.subtract)
            stt_done[i] = ctx.inc(c, "s_ve")
            sc.wait_ge(ctx.sem("s_ve"), stt_done[i])
            a = sc.activation(out=wa1[:, qs:qs + 1024], in_=wf[:, qs:qs + 1024],
                              func=ACT.Sign,
                              bias=consts_sb[:, C_NEGHALF:C_NEGHALF + 1], scale=1.0)
            sg1_done[i] = ctx.inc(a, "s_sc")
            sc.wait_ge(ctx.sem("s_sc"), sg1_done[i])
            a = sc.activation(out=wf[:, qs:qs + 1024], in_=wf[:, qs:qs + 1024],
                              func=ACT.Sign,
                              bias=consts_sb[:, C_HALF:C_HALF + 1], scale=1.0)
            sg2_done[i] = ctx.inc(a, "s_sc")
            ve.wait_ge(ctx.sem("s_sc"), sg2_done[i])
            if i >= 2 and (i - 2) in snk_dma:
                ve.wait_ge(ctx.sem("d_sy"), snk_dma[i - 2])
            c = ve.tensor_tensor(out=wa1[:, qs:qs + 1024], in0=wa1[:, qs:qs + 1024],
                                 in1=wf[:, qs:qs + 1024], op=AluOpType.add)
            c = ve.scalar_tensor_tensor(out=wf[:, qs:qs + 1024],
                                        in0=wa1[:, qs:qs + 1024], scalar=0.5,
                                        in1=wci[:, qs:qs + 1024],
                                        op0=AluOpType.mult, op1=AluOpType.add)
            c = ve.tensor_scalar(out=wq_ev[:, (i % 2) * 1024:(i % 2 + 1) * 1024],
                                 in0=wf[:, qs:qs + 1024],
                                 scalar1=1.0, scalar2=-1.0,
                                 op0=AluOpType.min, op1=AluOpType.max)
            fin_done[i] = ctx.inc(c, "s_ve")
            sink(i, wq_ev[:, (i % 2) * 1024:(i % 2 + 1) * 1024])

    def src_w1(i):
        dm, cs = i // 4, i % 4
        return io[P + "_W1T"][dm * 128:(dm + 1) * 128, cs * 1024:(cs + 1) * 1024]

    def src_w2(i):
        return io[P + "_W2T"][i * 128:(i + 1) * 128, :]

    def sink_w1(i, src):
        dm, cs = i // 4, i % 4
        c = ve.tensor_copy(w1q[:, dm * 4096 + cs * 1024: dm * 4096 + (cs + 1) * 1024], src)
        ctx.inc(c, "s_ve")

    snk2 = {}

    def sink_w2(i, src):
        ctx.after(sync, "s_ve")
        ctx.inc(sync.dma_start(out=io[P + "_w2qS"][i * 128:(i + 1) * 128, :], in_=src),
                "d_sy", 16)
        snk2[i] = ctx.val("d_sy")

    wquant(src_w1, 32, scl[:, 0:1], scl[:, 1:2], sink_w1, {})
    wquant(src_w2, 32, scl[:, 2:3], scl[:, 3:4], sink_w2, snk2)
    ctx.after(ve, "d_sy", "s_ve")
    c = ve.tensor_scalar(out=cw[:], in0=cw[:], scalar1=scl[:, 0:1], scalar2=None,
                         op0=AluOpType.mult)
    ctx.inc(c, "s_ve")
    stQ.close()

    stX = ExitStack()
    xt = stX.enter_context(nc.sbuf_tensor(P + "_xt", [128, 3 * 1024], F32))
    xq = stX.enter_context(nc.sbuf_tensor(P + "_xq", [128, 2 * 1024], BF16))
    qm = stX.enter_context(nc.sbuf_tensor(P + "_qm", [128, 8], F32))
    qci = stX.enter_context(nc.sbuf_tensor(P + "_qci", [128, 1024], I32))
    qf = stX.enter_context(nc.sbuf_tensor(P + "_qf", [128, 1024], F32))
    qa1 = stX.enter_context(nc.sbuf_tensor(P + "_qa1", [128, 1024], F32))
    rx = stX.enter_context(nc.sbuf_tensor(P + "_rx", [128, 40], F32))
    tev = stX.enter_context(nc.sbuf_tensor(P + "_tev", [128, 1024], BF16))
    ps_t = stX.enter_context(nc.psum_tensor(P + "_ps_t", [128, 2 * 1024], BF16))
    qb = {"m": qm, "ci": qci[:], "f": qf[:], "a1": qa1[:]}
    ld_x = [None] * NT
    qout = [None] * NT
    xmm_last = [None] * NT
    xtev_last = [None] * NT
    xdma_last = [None] * NT
    ctx.after(sync, "s_ve", "s_sc")
    for tl in range(NT):
        pb = tl % 3
        qs = (tl % 2) * 1024
        if tl >= 3:
            sync.wait_ge(ctx.sem("s_ve"), qout[tl - 3])
        ctx.inc(sync.dma_start(out=xt[:, pb * 1024:(pb + 1) * 1024],
                               in_=io[P + "_xin"][tl * 128:(tl + 1) * 128, :]),
                "d_sy", 16)
        ld_x[tl] = ctx.val("d_sy")
        ve.wait_ge(ctx.sem("d_sy"), ld_x[tl])
        if tl >= 2:
            ve.wait_ge(ctx.sem("s_te"), xmm_last[tl - 2])
        quant_rows(ctx, io, xt[:, pb * 1024:(pb + 1) * 1024], consts_sb,
                   qb, xq[:, qs:qs + 1024], rx[:, tl:tl + 1])
        if tl == 0 or tl == NT - 1:
            ve.wait_ge(ctx.sem("s_ve"), ctx.counts["s_ve" + ctx.suffix])
            c = ve.tensor_scalar(out=xq[:, qs:qs + 1024], in0=xq[:, qs:qs + 1024],
                                 scalar1=msk[:, 0:1] if tl == 0 else msk[:, 1:2],
                                 scalar2=None, op0=AluOpType.mult)
            ctx.inc(c, "s_ve")
        qout[tl] = ctx.val("s_ve")
        xt_mm = [None] * DC
        xt_ev = [None] * DC

        def emit_xtr(dc):
            hb = dc % 2
            m = te.transpose(ps_t[:, hb * 1024: hb * 1024 + 128],
                             xq[:, qs + dc * 128: qs + (dc + 1) * 128], io["ident_sb"])
            xt_mm[dc] = ctx.inc(m, "s_te")

        te.wait_ge(ctx.sem("s_ve"), qout[tl])
        if tl >= 1:
            te.wait_ge(ctx.sem("s_sc"), xtev_last[tl - 1])
        emit_xtr(0)
        for dc in range(DC):
            if dc + 1 < DC:
                if dc >= 1:
                    te.wait_ge(ctx.sem("s_sc"), xt_ev[dc - 1])
                emit_xtr(dc + 1)
            sc.wait_ge(ctx.sem("s_te"), xt_mm[dc])
            if dc == 0 and tl >= 1:
                sc.wait_ge(ctx.sem("d_sy"), xdma_last[tl - 1])
            a = sc.activation(out=tev[:, dc * 128:(dc + 1) * 128],
                              in_=ps_t[:, (dc % 2) * 1024: (dc % 2) * 1024 + 128], func=ACT.Copy)
            xt_ev[dc] = ctx.inc(a, "s_sc")
        xmm_last[tl] = xt_mm[DC - 1]
        xtev_last[tl] = xt_ev[DC - 1]
        ctx.after(sync, "s_sc")
        for dc in range(DC):
            ctx.inc(sync.dma_start(
                out=io[P + "_xqT"][dc * 128:(dc + 1) * 128, tl * 128:(tl + 1) * 128],
                in_=tev[:, dc * 128:(dc + 1) * 128]), "d_sy", 16)
        xdma_last[tl] = ctx.val("d_sy")
    stX.close()

    st1 = ExitStack()
    xqtb = st1.enter_context(nc.sbuf_tensor(P + "_xqTb", [128, 2 * 8 * 258], BF16))
    hTM = st1.enter_context(nc.sbuf_tensor(P + "_hTM", [128, 2 * 4096], F32))
    hqTM = st1.enter_context(nc.sbuf_tensor(P + "_hqTM", [128, 2 * 4096], BF16))
    cvb = st1.enter_context(nc.sbuf_tensor(P + "_cv", [128, 3 * 256], F32))
    slb = st1.enter_context(nc.sbuf_tensor(P + "_sl", [128, 3 * 256], F32))
    qm2 = st1.enter_context(nc.sbuf_tensor(P + "_qm2", [128, 8], F32))
    qci2 = st1.enter_context(nc.sbuf_tensor(P + "_qci2", [128, 4096], I32))
    qf2 = st1.enter_context(nc.sbuf_tensor(P + "_qf2", [128, 4096], F32))
    qa12 = st1.enter_context(nc.sbuf_tensor(P + "_qa12", [128, 4096], F32))
    rh = st1.enter_context(nc.sbuf_tensor(P + "_rh", [128, 2], F32))
    hqTe = st1.enter_context(nc.sbuf_tensor(P + "_hqTe", [128, 2 * 256], BF16))
    ps_h = st1.enter_context(nc.psum_tensor(P + "_ps_h", [128, 4 * 512], F32))
    ps_tt = st1.enter_context(nc.psum_tensor(P + "_ps_tt", [128, 2 * 512], F32))
    ps_tb = st1.enter_context(nc.psum_tensor(P + "_ps_tb", [128, 2 * 1024], BF16))
    qb2 = {"m": qm2, "ci": qci2[:], "f": qf2[:], "a1": qa12[:]}
    blk_lo = [max(blk * 256 - 1, 0) for blk in range(NBLK)]
    blk_hi = [min(blk * 256 + 257, TEXT) for blk in range(NBLK)]
    blk_nc = [blk_hi[b] - blk_lo[b] for b in range(NBLK)]
    blk_bs = [blk * 256 - blk_lo[blk] for blk in range(NBLK)]
    NG = NBLK * 32
    w1_done = [None] * NG
    conv_done = [None] * NG
    silu_done = [None] * NG
    tr_done = [None] * NG
    ev_done = [None] * NG
    load_done = [None] * NBLK
    hT_free = [None] * NBLK
    tr2_last = [None] * NBLK

    def emit_loads(blk):
        pb = blk % 2
        if blk >= 2:
            sync.wait_ge(ctx.sem("s_te"), w1_done[(blk - 2) * 32 + 31])
        for dc in range(DC):
            d = sync.dma_start(
                out=xqtb[:, (pb * 8 + dc) * 258:(pb * 8 + dc) * 258 + blk_nc[blk]],
                in_=io[P + "_xqT"][dc * 128:(dc + 1) * 128, blk_lo[blk]:blk_hi[blk]])
            ctx.inc(d, "d_sy", 16)
        load_done[blk] = ctx.val("d_sy")

    def MM(g):
        blk, hc = g // 32, g % 32
        pb = blk % 2
        bank = g % 4
        if hc == 0:
            te.wait_ge(ctx.sem("d_sy"), load_done[blk])
        if g >= 4:
            te.wait_ge(ctx.sem("s_ve"), conv_done[g - 4])
        for dc in range(DC):
            m = te.matmul(ps_h[:, bank * 512: bank * 512 + blk_nc[blk]],
                          w1q[:, dc * 4096 + hc * 128: dc * 4096 + (hc + 1) * 128],
                          xqtb[:, (pb * 8 + dc) * 258:(pb * 8 + dc) * 258 + blk_nc[blk]],
                          start=(dc == 0), stop=(dc == DC - 1))
        w1_done[g] = ctx.inc(m, "s_te")

    def CONV(g):
        blk, hc = g // 32, g % 32
        bank, slot = g % 4, g % 3
        base, ncols = blk_bs[blk], blk_nc[blk]
        ve.wait_ge(ctx.sem("s_te"), w1_done[g])
        if g >= 3:
            ve.wait_ge(ctx.sem("s_sc"), silu_done[g - 3])
        cv = cvb[:, slot * 256: slot * 256 + 256]
        c = ve.tensor_scalar(out=cv,
                             in0=ps_h[:, bank * 512 + base: bank * 512 + base + 256],
                             scalar1=cw[:, 3 * hc + 1: 3 * hc + 2], scalar2=None,
                             op0=AluOpType.mult)
        if base >= 1:
            c = ve.scalar_tensor_tensor(
                out=cv,
                in0=ps_h[:, bank * 512 + base - 1: bank * 512 + base + 255],
                scalar=cw[:, 3 * hc: 3 * hc + 1], in1=cv,
                op0=AluOpType.mult, op1=AluOpType.add)
        else:
            c = ve.scalar_tensor_tensor(
                out=cvb[:, slot * 256 + 1: slot * 256 + 256],
                in0=ps_h[:, bank * 512: bank * 512 + 255],
                scalar=cw[:, 3 * hc: 3 * hc + 1],
                in1=cvb[:, slot * 256 + 1: slot * 256 + 256],
                op0=AluOpType.mult, op1=AluOpType.add)
        w2w = min(256, ncols - base - 1)
        c = ve.scalar_tensor_tensor(
            out=cvb[:, slot * 256: slot * 256 + w2w],
            in0=ps_h[:, bank * 512 + base + 1: bank * 512 + base + 1 + w2w],
            scalar=cw[:, 3 * hc + 2: 3 * hc + 3],
            in1=cvb[:, slot * 256: slot * 256 + w2w],
            op0=AluOpType.mult, op1=AluOpType.add)
        conv_done[g] = ctx.inc(c, "s_ve")

    def SILU(g):
        blk, hc = g // 32, g % 32
        slot = g % 3
        sc.wait_ge(ctx.sem("s_ve"), conv_done[g])
        if g >= 3:
            sc.wait_ge(ctx.sem("s_te"), tr_done[g - 3])
        a = sc.activation(out=slb[:, slot * 256: slot * 256 + 256],
                          in_=cvb[:, slot * 256: slot * 256 + 256], func=ACT.Silu,
                          bias=cb[:, hc:hc + 1], scale=1.0)
        silu_done[g] = ctx.inc(a, "s_sc")

    def TR(g):
        slot, b2 = g % 3, g % 2
        te.wait_ge(ctx.sem("s_sc"), ev_done[g - 2] if g >= 2 else silu_done[g])
        for i in range(2):
            m = te.transpose(ps_tt[:, b2 * 512 + i * 128: b2 * 512 + (i + 1) * 128],
                             slb[:, slot * 256 + i * 128: slot * 256 + (i + 1) * 128],
                             io["identf_sb"])
        tr_done[g] = ctx.inc(m, "s_te")

    def EVAC(g):
        blk, hc = g // 32, g % 32
        b2 = g % 2
        sc.wait_ge(ctx.sem("s_te"), tr_done[g])
        if hc == 0 and blk >= 1:
            sc.wait_ge(ctx.sem("s_ve"), hT_free[blk - 1])
        for i in range(2):
            a = sc.activation(out=hTM[:, i * 4096 + hc * 128: i * 4096 + (hc + 1) * 128],
                              in_=ps_tt[:, b2 * 512 + i * 128: b2 * 512 + i * 128 + 128],
                              func=ACT.Copy)
        ev_done[g] = ctx.inc(a, "s_sc")

    def TAIL(blk):
        for i in range(2):
            if i == 0:
                ve.wait_ge(ctx.sem("s_sc"), ev_done[blk * 32 + 31])
                if blk >= 1:
                    ve.wait_ge(ctx.sem("s_te"), tr2_last[blk - 1])
            quant_rows(ctx, io, hTM[:, i * 4096:(i + 1) * 4096], consts_sb,
                       qb2, hqTM[:, i * 4096:(i + 1) * 4096], rh[:, i:i + 1])
        hT_free[blk] = ctx.val("s_ve")
        qdone = ctx.val("s_ve")
        t_mm = [None] * 64
        t_ev = [None] * 64
        t_dma = [None] * 64

        def TR2(k):
            i, hc = k // 32, k % 32
            hb = k % 2
            if k == 0:
                te.wait_ge(ctx.sem("s_ve"), qdone)
            if k >= 2:
                te.wait_ge(ctx.sem("s_sc"), t_ev[k - 2])
            m = te.transpose(ps_tb[:, hb * 1024: hb * 1024 + 128],
                             hqTM[:, i * 4096 + hc * 128: i * 4096 + (hc + 1) * 128],
                             io["ident_sb"])
            t_mm[k] = ctx.inc(m, "s_te")

        def EV2(k):
            hb = k % 2
            sc.wait_ge(ctx.sem("s_te"), t_mm[k])
            if k >= 2:
                sc.wait_ge(ctx.sem("d_sy"), t_dma[k - 2])
            a = sc.activation(out=hqTe[:, hb * 256: hb * 256 + 128],
                              in_=ps_tb[:, hb * 1024: hb * 1024 + 128], func=ACT.Copy)
            t_ev[k] = ctx.inc(a, "s_sc")

        def DMA2(k):
            i, hc = k // 32, k % 32
            hb = k % 2
            tl = blk * 2 + i
            sync.wait_ge(ctx.sem("s_sc"), t_ev[k])
            t_dma[k] = ctx.inc(sync.dma_start(
                out=io[P + "_hqT"][hc * 128:(hc + 1) * 128, tl * 128:(tl + 1) * 128],
                in_=hqTe[:, hb * 256: hb * 256 + 128]), "d_sy", 16)

        for s2 in range(64 + 3):
            if 0 <= s2 - 2 < 64:
                EV2(s2 - 2)
            if 0 <= s2 - 3 < 64:
                DMA2(s2 - 3)
            if s2 < 64:
                TR2(s2)
        tr2_last[blk] = t_mm[63]

    ctx.after(te, "s_ve", "s_sc")
    ctx.after(sc, "d_sy")
    emit_loads(0)
    for s in range(NG + 5):
        if s < NG:
            MM(s)
        if s >= 8 and (s - 8) % 32 == 0 and (s - 8) // 32 + 1 < NBLK:
            emit_loads((s - 8) // 32 + 1)
        if 0 <= s - 2 < NG:
            CONV(s - 2)
        if 0 <= s - 3 < NG:
            SILU(s - 3)
        if 0 <= s - 4 < NG:
            TR(s - 4)
        if 0 <= s - 5 < NG:
            EVAC(s - 5)
        if s >= 36 and (s - 36) % 32 == 0 and (s - 36) // 32 < NBLK:
            TAIL((s - 36) // 32)
    st1.close()

    st2 = ExitStack()
    w2q = st2.enter_context(nc.sbuf_tensor(P + "_w2q", [128, 32 * 1024], BF16))
    hqs = st2.enter_context(nc.sbuf_tensor(P + "_hqs", [128, 2 * 32 * 128], BF16))
    resb = st2.enter_context(nc.sbuf_tensor(P + "_res", [128, 1024], F32))
    x2t = st2.enter_context(nc.sbuf_tensor(P + "_x2t", [128, 1024], F32))
    lncol = st2.enter_context(nc.sbuf_tensor(P + "_ln", [128, 8], F32))
    sqd = st2.enter_context(nc.sbuf_tensor(P + "_sq", [128, 1024], F32))
    ps_o = st2.enter_context(nc.psum_tensor(P + "_ps_o", [128, 4 * 512], F32))
    xbf_t = st2.enter_context(nc.sbuf_tensor(P + "_xbf", [128, 1024], BF16))
    ctx.after(gp, "d_sy")
    for i in range(32):
        ctx.inc(gp.dma_start(out=w2q[:, i * 1024:(i + 1) * 1024],
                             in_=io[P + "_w2qS"][i * 128:(i + 1) * 128, :]), "d_gp", 16)
    ctx.after(te, "d_gp")
    dma_done_t = {}
    mm_done_t = {}

    def emit_hqs_load(tl):
        pb = tl % 2
        for hc in range(32):
            d = sync.dma_start(
                out=hqs[:, (pb * 32 + hc) * 128:(pb * 32 + hc + 1) * 128],
                in_=io[P + "_hqT"][hc * 128:(hc + 1) * 128, tl * 128:(tl + 1) * 128])
            ctx.inc(d, "d_sy", 16)
        dma_done_t[tl] = ctx.val("d_sy")

    emit_hqs_load(1)
    for tl in range(1, 33):
        pb = tl % 2
        if tl + 1 < 33:
            if tl >= 2:
                sync.wait_ge(ctx.sem("s_te"), mm_done_t[tl - 1])
            emit_hqs_load(tl + 1)
        te.wait_ge(ctx.sem("d_sy"), dma_done_t[tl])
        for hc in range(32):
            for vh in range(2):
                m = te.matmul(ps_o[:, (pb * 2 + vh) * 512:(pb * 2 + vh + 1) * 512],
                              hqs[:, (pb * 32 + hc) * 128:(pb * 32 + hc + 1) * 128],
                              w2q[:, hc * 1024 + vh * 512: hc * 1024 + (vh + 1) * 512],
                              start=(hc == 0), stop=(hc == 31))
        mm_done_t[tl] = ctx.inc(m, "s_te")
        ctx.after(sync, "s_ve")
        ctx.inc(sync.dma_start(out=resb[:], in_=io[P + "_xin"][tl * 128:(tl + 1) * 128, :]),
                "d_sy", 16)
        ctx.after(ve, "d_sy", "s_te")
        for vh in range(2):
            c = ve.scalar_tensor_tensor(
                out=x2t[:, vh * 512:(vh + 1) * 512],
                in0=ps_o[:, (pb * 2 + vh) * 512:(pb * 2 + vh + 1) * 512],
                scalar=scl[:, 2:3], in1=resb[:, vh * 512:(vh + 1) * 512],
                op0=AluOpType.mult, op1=AluOpType.add)
        ctx.inc(c, "s_ve")
        ln_epilogue(ctx, ve, sc, x2t[:], lncol, sqd[:], consts_sb, x2t[:])
        ctx.after(sync, "s_ve")
        ctx.inc(sync.dma_start(out=io[P + "_xout"][tl * 128:(tl + 1) * 128, :],
                               in_=x2t[:]), "d_sy", 16)
        if P + "_xout_bf" in io:
            ctx.after(sc, "s_ve", "d_sy")
            a = sc.activation(out=xbf_t[:], in_=x2t[:], func=ACT.Copy)
            ctx.inc(a, "s_sc")
            ctx.after(sync, "s_sc")
            ctx.inc(sync.dma_start(out=io[P + "_xout_bf"][(tl - 1) * 128: tl * 128, :],
                                   in_=xbf_t[:]), "d_sy", 16)
        ctx.after(ve, "d_sy")
        ctx.after(te, "s_ve")
    st2.close()
    stW.close()




def host_ext_slice(full_b, core):
    half = core % 2
    start = half * TOWN
    out = np.zeros((TEXT, full_b.shape[1]), np.float32)
    lo = start - 128
    s0, e0 = max(lo, 0), min(start + TOWN + 128, S)
    out[s0 - lo: e0 - lo] = full_b[s0:e0]
    return out


def host_stage_ffn_inputs(inp, core, lkey, pfx):
    half = core % 2
    w1 = inp[lkey + "_w1"]
    w2 = inp[lkey + "_w2"]
    cwv = inp[lkey + "_conv_w"][:, 0, :]
    cbv = inp[lkey + "_conv_b"]
    return {
        pfx + "_W1T": np.ascontiguousarray(w1.T),
        pfx + "_W2T": np.ascontiguousarray(w2.T),
        pfx + "_conv_w_cols": np.ascontiguousarray(
            cwv.reshape(32, 128, 3).transpose(1, 0, 2).reshape(128, 96)),
        pfx + "_conv_b_cols": np.ascontiguousarray(cbv.reshape(32, 128).T),
        "mask_lo": np.full((128, 1), 0.0 if half == 0 else 1.0, np.float32),
        "mask_hi": np.full((128, 1), 0.0 if half == 1 else 1.0, np.float32),
    }


DG_C = 8
TC1 = 512
TC2 = 256
NCH1 = S // TC1
NCH2 = S // TC2
SEG = TC2 + 1


def emit_stage_c1(ctx, io):
    from contextlib import ExitStack
    nc = ctx.nc
    sync, gp, ve, sc, te = (ctx.eng[k] for k in ("sync", "gpsimd", "vector", "scalar", "tensor"))
    consts_sb = io["consts_sb"]
    st = ExitStack()
    win = st.enter_context(nc.sbuf_tensor("c_win", [128, 8 * 2048], BF16))
    wx = st.enter_context(nc.sbuf_tensor("c_wx", [128, 8 * 33], BF16))
    ccw = st.enter_context(nc.sbuf_tensor("c_ccw", [128, 32], F32))
    ccb = st.enter_context(nc.sbuf_tensor("c_ccb", [128, 8], F32))
    xt = st.enter_context(nc.sbuf_tensor("c_xt", [128, 4 * 1024], BF16))
    xT = st.enter_context(nc.sbuf_tensor("c_xT", [128, 8 * 512], BF16))
    xib = st.enter_context(nc.sbuf_tensor("c_xib", [128, 8 * 515], F32))
    ub = st.enter_context(nc.sbuf_tensor("c_ub", [128, 8 * 512], F32))
    ubb = st.enter_context(nc.sbuf_tensor("c_ubb", [128, 8 * 512], BF16))
    zb = st.enter_context(nc.sbuf_tensor("c_zb", [128, 8 * 512], BF16))
    cvt = st.enter_context(nc.sbuf_tensor("c_cvt", [128, 512], F32))
    xde = st.enter_context(nc.sbuf_tensor("c_xde", [128, 512], F32))
    ps_t = st.enter_context(nc.psum_tensor("c_ps_t", [128, 2 * 1024], BF16))
    ps_p = st.enter_context(nc.psum_tensor("c_ps_p", [128, 2 * 512], F32))
    ps_x = st.enter_context(nc.psum_tensor("c_ps_x", [128, 512], F32))

    for dc in range(DC):
        ctx.inc(gp.dma_start(out=win[:, dc * 2048:(dc + 1) * 2048],
                             in_=io["c_WinT"][dc * 128:(dc + 1) * 128, :]), "d_gp", 16)
    ctx.inc(gp.dma_start(out=wx[:], in_=bass.AP(io["c_WxT"].tensor, 0,
                                                [[33, 128], [33 * 128, 8], [1, 33]])),
            "d_gp", 16)
    ctx.inc(sync.dma_start(out=ccw[:], in_=io["c_conv_w_cols"]), "d_sy", 16)
    ctx.inc(sync.dma_start(out=ccb[:], in_=io["c_conv_b_cols"]), "d_sy", 16)
    ctx.after(ve, "d_sy")
    c = ve.memset(xib[:], 0.0)
    ctx.inc(c, "s_ve")
    ctx.after(te, "d_gp", "d_sy")
    ctx.after(sc, "d_gp", "d_sy")

    for ch in range(NCH1):
        t0 = ch * TC1
        ctx.after(sync, "s_te")
        for i in range(4):
            ctx.inc(sync.dma_start(out=xt[:, i * 1024:(i + 1) * 1024],
                                   in_=io["c_x2bf"][t0 + i * 128: t0 + (i + 1) * 128, :]),
                    "d_sy", 16)
        ctx.after(te, "d_sy")
        for dc in range(DC):
            hb = dc % 2
            ctx.after(te, "s_ve")
            for i in range(4):
                m = te.transpose(ps_t[:, hb * 1024 + i * 128: hb * 1024 + (i + 1) * 128],
                                 xt[:, i * 1024 + dc * 128: i * 1024 + (dc + 1) * 128],
                                 io["ident_sb"])
            ctx.inc(m, "s_te")
            ctx.after(ve, "s_te")
            c = ve.tensor_copy(xT[:, dc * 512:(dc + 1) * 512], ps_t[:, hb * 1024: hb * 1024 + 512])
            ctx.inc(c, "s_ve")
        ctx.after(te, "s_ve")
        for fc in range(16):
            hb = fc % 2
            ctx.after(te, "s_ve", "s_sc")
            for dc in range(DC):
                m = te.matmul(ps_p[:, hb * 512:(hb + 1) * 512],
                              win[:, dc * 2048 + fc * 128: dc * 2048 + (fc + 1) * 128],
                              xT[:, dc * 512:(dc + 1) * 512],
                              start=(dc == 0), stop=(dc == DC - 1))
            ctx.inc(m, "s_te")
            if fc < 8:
                dg = fc
                ctx.after(ve, "s_te")
                c = ve.tensor_copy(xib[:, dg * 515 + 3: dg * 515 + 515],
                                   ps_p[:, hb * 512:(hb + 1) * 512])
                ctx.inc(c, "s_ve")
            else:
                dg = fc - 8
                ctx.after(sc, "s_te")
                a = sc.activation(out=zb[:, dg * 512:(dg + 1) * 512],
                                  in_=ps_p[:, hb * 512:(hb + 1) * 512], func=ACT.Copy)
                ctx.inc(a, "s_sc")
        if "c_xib_dbg" in io and ch == 1:
            ctx.after(sync, "s_ve", "s_sc")
            ctx.inc(sync.dma_start(out=io["c_xib_dbg"], in_=xib[:]), "d_sy", 16)
        for dg in range(DG_C):
            ctx.after(ve, "s_sc")
            c = ve.tensor_scalar(out=cvt[:], in0=xib[:, dg * 515 + 3: dg * 515 + 515],
                                 scalar1=ccw[:, dg * 4 + 3: dg * 4 + 4], scalar2=None,
                                 op0=AluOpType.mult)
            for j in range(3):
                c = ve.scalar_tensor_tensor(
                    out=cvt[:], in0=xib[:, dg * 515 + j: dg * 515 + j + 512],
                    scalar=ccw[:, dg * 4 + j: dg * 4 + j + 1], in1=cvt[:],
                    op0=AluOpType.mult, op1=AluOpType.add)
            ctx.inc(c, "s_ve")
            ctx.after(sc, "s_ve", "d_sy")
            a = sc.activation(out=ub[:, dg * 512:(dg + 1) * 512], in_=cvt[:],
                              func=ACT.Silu, bias=ccb[:, dg:dg + 1], scale=1.0)
            ctx.inc(a, "s_sc")
        c = ve.tensor_copy(bass.AP(xib.ap().tensor, 0, [[8 * 515, 128], [515, 8], [1, 3]]),
                           bass.AP(xib.ap().tensor, 512, [[8 * 515, 128], [515, 8], [1, 3]]))
        ctx.inc(c, "s_ve")
        ctx.after(sc, "s_sc")
        a = sc.activation(out=ubb[:], in_=ub[:], func=ACT.Copy)
        ctx.inc(a, "s_sc")
        ctx.after(te, "s_sc")
        for dg in range(DG_C):
            m = te.matmul(ps_x[0:33, 0:512], wx[:, dg * 33:(dg + 1) * 33],
                          ubb[:, dg * 512:(dg + 1) * 512],
                          start=(dg == 0), stop=(dg == DG_C - 1))
        ctx.inc(m, "s_te")
        ctx.after(ve, "s_te")
        c = ve.tensor_copy(xde[0:33, :], ps_x[0:33, 0:512])
        ctx.inc(c, "s_ve")
        ctx.after(sync, "s_sc", "s_ve")
        for dg in range(DG_C):
            ctx.inc(sync.dma_start(out=io["c_u"][dg * 128:(dg + 1) * 128, t0:t0 + TC1],
                                   in_=ub[:, dg * 512:(dg + 1) * 512]), "d_sy", 16)
            ctx.inc(sync.dma_start(out=io["c_z"][dg * 128:(dg + 1) * 128, t0:t0 + TC1],
                                   in_=zb[:, dg * 512:(dg + 1) * 512]), "d_sy", 16)
        ctx.inc(sync.dma_start(out=io["c_xdbl"][0:33, t0:t0 + TC1], in_=xde[0:33, :]),
                "d_sy", 16)
        ctx.after(te, "s_ve")
    st.close()


def emit_stage_c2(ctx, io):
    from contextlib import ExitStack
    nc = ctx.nc
    sync, gp, ve, sc, te = (ctx.eng[k] for k in ("sync", "gpsimd", "vector", "scalar", "tensor"))
    consts_sb = io["consts_sb"]
    st = ExitStack()
    wout = st.enter_context(nc.sbuf_tensor("c_wout", [128, 8 * 1024], BF16))
    acols = st.enter_context(nc.sbuf_tensor("c_acols", [128, 8 * 16], F32))
    dtc = st.enter_context(nc.sbuf_tensor("c_dtc", [128, 16], F32))
    ub2 = st.enter_context(nc.sbuf_tensor("c_ub2", [128, 2 * 8 * TC2], F32))
    zb2 = st.enter_context(nc.sbuf_tensor("c_zb2", [128, 2 * 8 * TC2], BF16))
    xdrows = st.enter_context(nc.sbuf_tensor("c_xdrows", [1, 33 * TC2], BF16))
    dlt = st.enter_context(nc.sbuf_tensor("c_dlt", [128, 2 * 8 * TC2], F32))
    dub = st.enter_context(nc.sbuf_tensor("c_dub", [128, TC2], F32))
    brep = st.enter_context(nc.sbuf_tensor("c_brep", [128, 16 * SEG], BF16))
    crep = st.enter_context(nc.sbuf_tensor("c_crep", [128, 16 * SEG], BF16))
    abuf = st.enter_context(nc.sbuf_tensor("c_abuf", [128, 2 * 16 * SEG], F32))
    bbuf = st.enter_context(nc.sbuf_tensor("c_bbuf", [128, 16 * SEG], F32))
    hbuf = st.enter_context(nc.sbuf_tensor("c_hbuf", [128, 16 * SEG], F32))
    stc = st.enter_context(nc.sbuf_tensor("c_stc", [128, 8 * 16], F32))
    yb = st.enter_context(nc.sbuf_tensor("c_yb", [128, 8 * TC2], F32))
    spw = st.enter_context(nc.sbuf_tensor("c_spw", [128, TC2], F32))
    ygb = st.enter_context(nc.sbuf_tensor("c_ygb", [128, 8 * TC2], BF16))
    oev = st.enter_context(nc.sbuf_tensor("c_oev", [128, 2 * 1024], BF16))
    ps_r = st.enter_context(nc.psum_tensor("c_ps_r", [128, 2 * 512], F32))
    ps_d = st.enter_context(nc.psum_tensor("c_ps_d", [128, 512], F32))
    ps_o = st.enter_context(nc.psum_tensor("c_ps_o", [128, 4 * 512], F32))

    for dg in range(DG_C):
        ctx.inc(gp.dma_start(out=wout[:, dg * 1024:(dg + 1) * 1024],
                             in_=io["c_WoutT"][dg * 128:(dg + 1) * 128, :]), "d_gp", 16)
    ctx.inc(sync.dma_start(out=acols[:], in_=io["c_Acols"]), "d_sy", 16)
    ctx.inc(sync.dma_start(out=dtc[:], in_=io["c_dt_cols"]), "d_sy", 16)
    ctx.after(ve, "d_sy")
    c = ve.memset(abuf[:], 0.0)
    c = ve.memset(stc[:], 0.0)
    c = ve.memset(bbuf[:], 0.0)
    c = ve.memset(crep[:], 0.0)
    ctx.inc(c, "s_ve")
    ctx.after(te, "d_gp", "d_sy")
    ctx.after(sc, "d_sy", "s_ve")

    NCH = NCH2
    load_done = [None] * NCH
    xd_done = [None] * NCH
    dpre_mm = [None] * NCH
    dlt_done = [None] * NCH
    rep_ev_last = [None] * NCH
    rep_mm_last = [None] * NCH
    dg_end = [None] * NCH
    chunk_ve_end = [None] * NCH
    oev_dma = [None] * NCH
    oev_ev = [None] * NCH
    wout_done = [None] * NCH

    def HEAD(ch):
        pc = ch % 2
        t0 = ch * TC2
        if ch >= 2:
            sync.wait_ge(ctx.sem("s_ve"), chunk_ve_end[ch - 2])
        for dg in range(DG_C):
            ctx.inc(sync.dma_start(
                out=ub2[:, (pc * 8 + dg) * TC2:(pc * 8 + dg + 1) * TC2],
                in_=io["c_u"][dg * 128:(dg + 1) * 128, t0:t0 + TC2]), "d_sy", 16)
            ctx.inc(sync.dma_start(
                out=zb2[:, (pc * 8 + dg) * TC2:(pc * 8 + dg + 1) * TC2],
                in_=io["c_z"][dg * 128:(dg + 1) * 128, t0:t0 + TC2]), "d_sy", 16)
        load_done[ch] = ctx.val("d_sy")
        if ch >= 1:
            gp.wait_ge(ctx.sem("s_te"), rep_mm_last[ch - 1])
        ctx.inc(gp.dma_start(out=xdrows[:],
                             in_=bass.AP(io["c_xdbl_sum"].tensor, t0,
                                         [[1, 1], [S, 33], [1, TC2]])), "d_gp", 16)
        xd_done[ch] = ctx.val("d_gp")
        te.wait_ge(ctx.sem("d_gp"), xd_done[ch])
        if ch >= 1:
            te.wait_ge(ctx.sem("s_sc"), dlt_done[ch - 1])
            te.wait_ge(ctx.sem("s_sc"), rep_ev_last[ch - 1])
        m = te.matmul(ps_d[:, 0:TC2], io["ones_col_sb"], xdrows[0:1, 0:TC2],
                      start=True, stop=True)
        dpre_mm[ch] = ctx.inc(m, "s_te")
        sc.wait_ge(ctx.sem("s_te"), dpre_mm[ch])
        for dg in range(DG_C):
            a = sc.activation(out=spw[:, 0:TC2], in_=ps_d[:, 0:TC2],
                              func=ACT.Exp, scale=dtc[:, dg:dg + 1],
                              bias=dtc[:, 8 + dg: 9 + dg])
            ctx.inc(a, "s_sc")
            sc.wait_ge(ctx.sem("s_sc"), ctx.counts["s_sc" + ctx.suffix])
            a = sc.activation(out=dlt[:, (pc * 8 + dg) * TC2:(pc * 8 + dg + 1) * TC2],
                              in_=spw[:, 0:TC2], func=ACT.Ln, scale=1.0,
                              bias=consts_sb[:, C_ONE:C_ONE + 1])
            ctx.inc(a, "s_sc")
        dlt_done[ch] = ctx.val("s_sc")
        r_mm = [None] * 32
        r_ev = [None] * 32

        def emit_rep(n):
            hb = n % 2
            m = te.matmul(ps_r[:, hb * 512: hb * 512 + TC2], io["ones_col_sb"],
                          xdrows[0:1, (1 + n) * TC2:(2 + n) * TC2], start=True, stop=True)
            r_mm[n] = ctx.inc(m, "s_te")

        emit_rep(0)
        for n in range(32):
            if n + 1 < 32:
                if n >= 1:
                    te.wait_ge(ctx.sem("s_sc"), r_ev[n - 1])
                emit_rep(n + 1)
            sc.wait_ge(ctx.sem("s_te"), r_mm[n])
            if n == 0 and ch >= 1:
                sc.wait_ge(ctx.sem("s_ve"), dg_end[ch - 1])
            dst = brep if n < 16 else crep
            seg = n if n < 16 else n - 16
            a = sc.activation(out=dst[:, seg * SEG + 1: seg * SEG + 1 + TC2],
                              in_=ps_r[:, (n % 2) * 512: (n % 2) * 512 + TC2], func=ACT.Copy)
            r_ev[n] = ctx.inc(a, "s_sc")
        rep_ev_last[ch] = r_ev[31]
        rep_mm_last[ch] = r_mm[31]

    def DGLOOP(ch):
        pc = ch % 2
        a_done = [None] * DG_C
        ve_done = [None] * DG_C

        def emit_abuild(dg):
            par = dg % 2
            for n in range(16):
                a = sc.activation(out=abuf[:, par * 16 * SEG + n * SEG + 1:
                                           par * 16 * SEG + n * SEG + 1 + TC2],
                                  in_=dlt[:, (pc * 8 + dg) * TC2:(pc * 8 + dg + 1) * TC2],
                                  func=ACT.Exp,
                                  scale=acols[:, dg * 16 + n: dg * 16 + n + 1])
            a_done[dg] = ctx.inc(a, "s_sc")

        ctx.after(sc, "s_ve")
        emit_abuild(0)
        for dg in range(DG_C):
            par = dg % 2
            ab = par * 16 * SEG
            if dg + 1 < DG_C:
                if dg >= 1:
                    sc.wait_ge(ctx.sem("s_ve"), ve_done[dg - 1])
                emit_abuild(dg + 1)
            if dg == 0:
                ve.wait_ge(ctx.sem("d_sy"), load_done[ch])
                ve.wait_ge(ctx.sem("s_sc"), rep_ev_last[ch])
            c = ve.tensor_tensor(out=dub[:, 0:TC2],
                                 in0=dlt[:, (pc * 8 + dg) * TC2:(pc * 8 + dg + 1) * TC2],
                                 in1=ub2[:, (pc * 8 + dg) * TC2:(pc * 8 + dg + 1) * TC2],
                                 op=AluOpType.mult)
            for n in range(16):
                c = ve.tensor_tensor(out=bbuf[:, n * SEG + 1: n * SEG + 1 + TC2],
                                     in0=dub[:, 0:TC2],
                                     in1=brep[:, n * SEG + 1: n * SEG + 1 + TC2],
                                     op=AluOpType.mult)
            c = ve.tensor_copy(bass.AP(bbuf.ap().tensor, 0, [[16 * SEG, 128], [SEG, 16]]),
                               stc[:, dg * 16:(dg + 1) * 16])
            ctx.inc(c, "s_ve")
            ve.wait_ge(ctx.sem("s_ve"), ctx.counts["s_ve" + ctx.suffix])
            ve.wait_ge(ctx.sem("s_sc"), a_done[dg])
            c = ve.tensor_tensor_scan(out=hbuf[:],
                                      data0=abuf[:, ab: ab + 16 * SEG], data1=bbuf[:],
                                      initial=0.0, op0=AluOpType.mult, op1=AluOpType.add)
            ctx.inc(c, "s_ve")
            ve.wait_ge(ctx.sem("s_ve"), ctx.counts["s_ve" + ctx.suffix])
            c = ve.tensor_copy(stc[:, dg * 16:(dg + 1) * 16],
                               bass.AP(hbuf.ap().tensor, TC2, [[16 * SEG, 128], [SEG, 16]]))
            ctx.inc(c, "s_ve")
            c = ve.tensor_tensor(out=bbuf[:], in0=hbuf[:], in1=crep[:],
                                 op=AluOpType.mult)
            c = ve.tensor_tensor(out=bbuf[:, 0:8 * SEG], in0=bbuf[:, 0:8 * SEG],
                                 in1=bbuf[:, 8 * SEG:16 * SEG], op=AluOpType.add)
            c = ve.tensor_tensor(out=bbuf[:, 0:4 * SEG], in0=bbuf[:, 0:4 * SEG],
                                 in1=bbuf[:, 4 * SEG:8 * SEG], op=AluOpType.add)
            c = ve.tensor_tensor(out=bbuf[:, 0:2 * SEG], in0=bbuf[:, 0:2 * SEG],
                                 in1=bbuf[:, 2 * SEG:4 * SEG], op=AluOpType.add)
            c = ve.tensor_tensor(out=yb[:, dg * TC2:(dg + 1) * TC2],
                                 in0=bbuf[:, 1:1 + TC2],
                                 in1=bbuf[:, SEG + 1:SEG + 1 + TC2], op=AluOpType.add)
            ve_done[dg] = ctx.inc(c, "s_ve")
        dg_end[ch] = ve_done[DG_C - 1]

    HEAD(0)
    for ch in range(NCH):
        pc = ch % 2
        t0 = ch * TC2
        DGLOOP(ch)
        if ch + 1 < NCH:
            HEAD(ch + 1)
        sc.wait_ge(ctx.sem("s_ve"), dg_end[ch])
        sc.wait_ge(ctx.sem("d_sy"), load_done[ch])
        for dg in range(DG_C):
            a = sc.activation(out=zb2[:, (pc * 8 + dg) * TC2:(pc * 8 + dg + 1) * TC2],
                              in_=zb2[:, (pc * 8 + dg) * TC2:(pc * 8 + dg + 1) * TC2],
                              func=ACT.Silu)
        ctx.inc(a, "s_sc")
        ctx.after(ve, "s_sc")
        if ch >= 1:
            ve.wait_ge(ctx.sem("s_te"), wout_done[ch - 1])
        for dg in range(DG_C):
            c = ve.tensor_tensor(out=ygb[:, dg * TC2:(dg + 1) * TC2],
                                 in0=yb[:, dg * TC2:(dg + 1) * TC2],
                                 in1=zb2[:, (pc * 8 + dg) * TC2:(pc * 8 + dg + 1) * TC2],
                                 op=AluOpType.mult)
        ygb_done = ctx.inc(c, "s_ve")
        chunk_ve_end[ch] = ygb_done
        te.wait_ge(ctx.sem("s_ve"), ygb_done)
        if ch >= 1:
            te.wait_ge(ctx.sem("s_sc"), oev_ev[ch - 1])
        ntt = TC2 // 128
        for tt in range(ntt):
            for vh in range(2):
                for dg in range(DG_C):
                    m = te.matmul(ps_o[:, (tt % 2 * 2 + vh) * 512:(tt % 2 * 2 + vh + 1) * 512],
                                  ygb[:, dg * TC2 + tt * 128: dg * TC2 + (tt + 1) * 128],
                                  wout[:, dg * 1024 + vh * 512: dg * 1024 + (vh + 1) * 512],
                                  start=(dg == 0), stop=(dg == DG_C - 1))
            ctx.inc(m, "s_te")
            ctx.after(sc, "s_te")
            if ch >= 1:
                sc.wait_ge(ctx.sem("d_sy"), oev_dma[ch - 1])
            a = sc.activation(out=oev[:, (tt % 2) * 1024:(tt % 2 + 1) * 1024],
                              in_=ps_o[:, (tt % 2 * 2) * 512:(tt % 2 * 2 + 2) * 512],
                              func=ACT.Copy)
            ctx.inc(a, "s_sc")
            ctx.after(sync, "s_sc")
            ctx.inc(sync.dma_start(
                out=io["c_part"][t0 + tt * 128: t0 + (tt + 1) * 128, :],
                in_=oev[:, (tt % 2) * 1024:(tt % 2 + 1) * 1024]), "d_sy", 16)
        wout_done[ch] = ctx.val("s_te")
        oev_dma[ch] = ctx.val("d_sy")
        oev_ev[ch] = ctx.val("s_sc")
    st.close()


def host_stage_c_inputs(inp, core):
    b, half = core // 2, core % 2
    ch0 = half * 1024
    Win = inp["L1_attn_W_in"]
    WinT = Win.T
    Win_sl = np.concatenate([WinT[:, ch0:ch0 + 1024],
                             WinT[:, 2048 + ch0:2048 + ch0 + 1024]], axis=1)
    cwv = inp["L1_attn_conv_w"][ch0:ch0 + 1024, 0, :]
    cbv = inp["L1_attn_conv_b"][ch0:ch0 + 1024]
    Wx = inp["L1_attn_W_x"]
    WxT_sl = Wx.T[ch0:ch0 + 1024, :]
    A = inp["L1_attn_A"][ch0:ch0 + 1024]
    Wdt = inp["L1_attn_W_dt"][ch0:ch0 + 1024, 0]
    bdt = inp["L1_attn_b_dt"][ch0:ch0 + 1024]
    Wout = inp["L1_attn_W_out"]
    WoutT_sl = Wout.T[ch0:ch0 + 1024, :]
    dt_cols = np.zeros((128, 16), np.float32)
    dt_cols[:, 0:8] = Wdt.reshape(8, 128).T
    dt_cols[:, 8:16] = bdt.reshape(8, 128).T
    return {
        "c_WinT": np.ascontiguousarray(Win_sl),
        "c_WxT": np.ascontiguousarray(WxT_sl),
        "c_conv_w_cols": np.ascontiguousarray(
            cwv.reshape(8, 128, 4).transpose(1, 0, 2).reshape(128, 32)),
        "c_conv_b_cols": np.ascontiguousarray(cbv.reshape(8, 128).T),
        "c_Acols": np.ascontiguousarray(
            A.reshape(8, 128, 16).transpose(1, 0, 2).reshape(128, 128)),
        "c_dt_cols": dt_cols,
        "c_WoutT": np.ascontiguousarray(WoutT_sl),
    }


def emit_stage_d_prelude(ctx, io):
    from contextlib import ExitStack
    nc = ctx.nc
    sync, gp, ve, sc, te = (ctx.eng[k] for k in ("sync", "gpsimd", "vector", "scalar", "tensor"))
    consts_sb = io["consts_sb"]
    st = ExitStack()
    x2t = st.enter_context(nc.sbuf_tensor("dp_x2t", [128, 2 * 1024], F32))
    p0t = st.enter_context(nc.sbuf_tensor("d_p0t", [128, 2 * 1024], F32))
    p1t = st.enter_context(nc.sbuf_tensor("d_p1t", [128, 2 * 1024], F32))
    x3t = st.enter_context(nc.sbuf_tensor("d_x3t", [128, 1024], F32))
    lncol = st.enter_context(nc.sbuf_tensor("d_lnc", [128, 8], F32))
    sqd = st.enter_context(nc.sbuf_tensor("d_sqd", [128, 1024], F32))
    for tl in range(NT):
        pb = tl % 2
        ctx.after(sync, "s_ve", "s_sc")
        for buf, key in ((x2t, "d_x2ext"), (p0t, "d_p0"), (p1t, "d_p1")):
            ctx.inc(sync.dma_start(out=buf[:, pb * 1024:(pb + 1) * 1024],
                                   in_=io[key][tl * 128:(tl + 1) * 128, :]), "d_sy", 16)
        ctx.after(ve, "d_sy")
        c = ve.tensor_tensor(out=x3t[:], in0=p0t[:, pb * 1024:(pb + 1) * 1024],
                             in1=p1t[:, pb * 1024:(pb + 1) * 1024], op=AluOpType.add)
        c = ve.tensor_tensor(out=x3t[:], in0=x3t[:],
                             in1=x2t[:, pb * 1024:(pb + 1) * 1024], op=AluOpType.add)
        ctx.inc(c, "s_ve")
        ln_epilogue(ctx, ve, sc, x3t[:], lncol, sqd[:], consts_sb, x3t[:])
        ctx.after(sync, "s_ve")
        ctx.inc(sync.dma_start(out=io["d_x3"][tl * 128:(tl + 1) * 128, :], in_=x3t[:]),
                "d_sy", 16)
        ctx.after(ve, "d_sy")
    st.close()


def emit_stage_d_prelude_fused(ctx, io):
    from contextlib import ExitStack
    nc = ctx.nc
    sync, gp, ve, sc, te = (ctx.eng[k] for k in ("sync", "gpsimd", "vector", "scalar", "tensor"))
    consts_sb = io["consts_sb"]
    st = ExitStack()
    x2t = st.enter_context(nc.sbuf_tensor("dp_x2t", [128, 2 * 1024], F32))
    p0t = st.enter_context(nc.sbuf_tensor("dp_p0t", [128, 2 * 1024], F32))
    x3t = st.enter_context(nc.sbuf_tensor("dp_x3t", [128, 1024], F32))
    lncol = st.enter_context(nc.sbuf_tensor("dp_lnc", [128, 8], F32))
    sqd = st.enter_context(nc.sbuf_tensor("dp_sqd", [128, 1024], F32))
    for tl in range(NT):
        pb = tl % 2
        ctx.after(sync, "s_ve", "s_sc")
        ctx.after(gp, "s_ve", "s_sc")
        if tl == 0:
            ctx.inc(gp.dma_start(out=x2t[:, pb * 1024:(pb + 1) * 1024],
                                 in_=io["c_x2bf"][3968:4096, :]), "d_gp", 16)
            ctx.inc(gp.dma_start(out=p0t[:, pb * 1024:(pb + 1) * 1024],
                                 in_=io["c_edge"][0:128, :]), "d_gp", 16)
        elif tl == NT - 1:
            ctx.inc(gp.dma_start(out=x2t[:, pb * 1024:(pb + 1) * 1024],
                                 in_=io["c_x2bf"][4096:4224, :]), "d_gp", 16)
            ctx.inc(gp.dma_start(out=p0t[:, pb * 1024:(pb + 1) * 1024],
                                 in_=io["c_edge"][128:256, :]), "d_gp", 16)
        else:
            ctx.inc(sync.dma_start(out=x2t[:, pb * 1024:(pb + 1) * 1024],
                                   in_=io["b_xout"][tl * 128:(tl + 1) * 128, :]), "d_sy", 16)
            ctx.inc(gp.dma_start(out=p0t[:, pb * 1024:(pb + 1) * 1024],
                                 in_=io["c_rsout"][(tl - 1) * 128: tl * 128, :]), "d_gp", 16)
        ctx.after(ve, "d_sy", "d_gp")
        c = ve.tensor_tensor(out=x3t[:], in0=x2t[:, pb * 1024:(pb + 1) * 1024],
                             in1=p0t[:, pb * 1024:(pb + 1) * 1024], op=AluOpType.add)
        ctx.inc(c, "s_ve")
        ln_epilogue(ctx, ve, sc, x3t[:], lncol, sqd[:], consts_sb, x3t[:])
        ctx.after(sync, "s_ve")
        ctx.inc(sync.dma_start(out=io["d_x3"][tl * 128:(tl + 1) * 128, :], in_=x3t[:]),
                "d_sy", 16)
        ctx.after(ve, "d_sy")
    st.close()


from contextlib import ExitStack


def _common_inputs():
    import ml_dtypes
    return {
        "consts": CONSTS_NP,
        "ident": np.eye(128, dtype=ml_dtypes.bfloat16),
        "identf": np.eye(128, dtype=np.float32),
        "ones_col": np.ones((1, 128), ml_dtypes.bfloat16),
        "ones_kcol": np.ones((128, 1), ml_dtypes.bfloat16),
        "onesf_row": np.ones((1, 128), np.float32),
    }


def _build_fused():
    nc = bass.Bass("TRN2", target_bir_lowering=False, detect_race_conditions=False)
    io = {}
    EI, IN, EO = "ExternalInput", "Internal", "ExternalOutput"

    def dram(name, shape, dt, kind):
        io[name] = nc.dram_tensor(name, shape, dt, kind=kind).ap()

    dram("t_batch", [S, D], F32, EI)
    dram("t_own", [TEXT, D], F32, EI)
    dram("WqkvT", [D, 3072], F32, EI)
    dram("bqkv_cols", [128, 24], F32, EI)
    dram("bv_row", [1, 1024], F32, EI)
    dram("Kscr", [D, S], BF16, IN)
    dram("Vscr", [S, D], BF16, IN)
    dram("Qscr", [D, TEXT], BF16, IN)
    dram("Oscr", [TEXT, D], F32, IN)
    dram("x1", [TEXT, D], F32, IN)
    dram("b_W1T", [1024, 4096], F32, EI)
    dram("b_W2T", [4096, 1024], F32, EI)
    dram("b_conv_w_cols", [128, 96], F32, EI)
    dram("b_conv_b_cols", [128, 32], F32, EI)
    dram("mask_lo", [128, 1], F32, EI)
    dram("mask_hi", [128, 1], F32, EI)
    dram("b_xqT", [1024, TEXT], BF16, IN)
    dram("b_w2qS", [4096, 1024], BF16, IN)
    dram("b_hqT", [4096, TEXT], BF16, IN)
    dram("b_xout", [TEXT, D], F32, IN)
    dram("b_xout_bf", [TOWN, D], BF16, IN)
    io["b_xin"] = io["x1"]
    dram("c_x2bf", [S, D], BF16, IN)
    dram("c_WinT", [1024, 2048], F32, EI)
    dram("c_WxT", [1024, 33], F32, EI)
    dram("c_conv_w_cols", [128, 32], F32, EI)
    dram("c_conv_b_cols", [128, 8], F32, EI)
    dram("c_Acols", [128, 128], F32, EI)
    dram("c_dt_cols", [128, 16], F32, EI)
    dram("c_WoutT", [1024, 1024], F32, EI)
    dram("c_u", [1024, S], F32, IN)
    dram("c_z", [1024, S], BF16, IN)
    dram("c_xdbl", [33, S], F32, IN)
    dram("c_xdbl_sum", [33, S], F32, IN)
    dram("c_part", [S, D], BF16, IN)
    dram("c_rsout", [TOWN, D], BF16, IN)
    dram("c_edgein", [256, D], BF16, IN)
    dram("c_edge", [256, D], BF16, IN)
    dram("d_W1T", [1024, 4096], F32, EI)
    dram("d_W2T", [4096, 1024], F32, EI)
    dram("d_conv_w_cols", [128, 96], F32, EI)
    dram("d_conv_b_cols", [128, 32], F32, EI)
    dram("d_x3", [TEXT, D], F32, IN)
    dram("d_xqT", [1024, TEXT], BF16, IN)
    dram("d_w2qS", [4096, 1024], BF16, IN)
    dram("d_hqT", [4096, TEXT], BF16, IN)
    dram("d_xout", [TEXT, D], F32, EO)
    io["d_xin"] = io["d_x3"]
    declare_common(nc, io)

    ctx = Ctx(nc)
    ctx.eng = {"sync": nc.sync, "gpsimd": nc.gpsimd, "vector": nc.vector,
               "scalar": nc.scalar, "tensor": nc.tensor}
    pairs = [[0, 1], [2, 3], [4, 5], [6, 7]]
    with ExitStack() as stack:
        ctx._stack = stack
        with nc.Block():
            load_common(ctx, io)
            ctx.set_stage("a")
            with nc.named_scope("stageA"):
                emit_stage_a(ctx, io)
            ctx.stage_barrier()
            ctx.set_stage("b")
            with nc.named_scope("stageB"):
                emit_stage_ffn(ctx, io, "b")
            ctx.stage_barrier()
            ctx.set_stage("c")
            gp = ctx.eng["gpsimd"]
            g = gp.collective_compute("AllGather", mybir.AluOpType.bypass,
                                      replica_groups=pairs,
                                      ins=[io["b_xout_bf"]], outs=[io["c_x2bf"]])
            ctx.inc(g, "s_gp")
            for e in ctx.eng.values():
                ctx.after(e, "s_gp")
            with nc.named_scope("stageC1"):
                emit_stage_c1(ctx, io)
            ctx.after(gp, "d_sy")
            g = gp.collective_compute("AllReduce", mybir.AluOpType.add,
                                      replica_groups=pairs,
                                      ins=[io["c_xdbl"]], outs=[io["c_xdbl_sum"]])
            ctx.inc(g, "s_gp")
            for e in ctx.eng.values():
                ctx.after(e, "s_gp")
            with nc.named_scope("stageC2"):
                emit_stage_c2(ctx, io)
            ctx.stage_barrier()
            ctx.set_stage("d")
            sync = ctx.eng["sync"]
            ctx.inc(sync.dma_start(out=io["c_edgein"], in_=io["c_part"][3968:4224, :]),
                    "d_sy", 16)
            ctx.after(gp, "d_sy")
            g = gp.collective_compute("ReduceScatter", mybir.AluOpType.add,
                                      replica_groups=pairs,
                                      ins=[io["c_part"]], outs=[io["c_rsout"]])
            ctx.inc(g, "s_gp")
            g = gp.collective_compute("AllReduce", mybir.AluOpType.add,
                                      replica_groups=pairs,
                                      ins=[io["c_edgein"]], outs=[io["c_edge"]])
            ctx.inc(g, "s_gp")
            for e in ctx.eng.values():
                ctx.after(e, "s_gp")
            with nc.named_scope("stageDp"):
                emit_stage_d_prelude_fused(ctx, io)
            with nc.named_scope("stageD"):
                emit_stage_ffn(ctx, io, "d")
            finish(ctx)
    return nc


def kernel(**inputs):
    from concourse.bass_utils import run_bass_kernel_spmd
    common = _common_inputs()
    cores = list(range(8))
    nc = _build_fused()
    maps = []
    for c in cores:
        m = {}
        m.update(host_stage_a_inputs(inputs, c))
        m.update(host_stage_ffn_inputs(inputs, c, "L0_ffn", "b"))
        m.update(host_stage_c_inputs(inputs, c))
        md = host_stage_ffn_inputs(inputs, c, "L1_ffn", "d")
        m["d_W1T"] = md["d_W1T"]
        m["d_W2T"] = md["d_W2T"]
        m["d_conv_w_cols"] = md["d_conv_w_cols"]
        m["d_conv_b_cols"] = md["d_conv_b_cols"]
        m.update(common)
        maps.append(m)
    res = run_bass_kernel_spmd(nc, maps, core_ids=cores)
    out = np.empty((4, S, D), np.float32)
    for c in cores:
        b, half = c // 2, c % 2
        out[b, half * TOWN:(half + 1) * TOWN] = \
            np.asarray(res.results[c]["d_xout"])[128:128 + TOWN]
    return out
